# version 14
# baseline (speedup 1.0000x reference)
import numpy as np
import ml_dtypes
from contextlib import ExitStack

import concourse.bass as bass
import concourse.mybir as mybir
from concourse.bass_utils import run_bass_kernel_spmd

F32 = mybir.dt.float32
BF16 = mybir.dt.bfloat16
FP16 = mybir.dt.float16
U16 = mybir.dt.uint16
AF = mybir.ActivationFunctionType
ALU = mybir.AluOpType
NPBF16 = ml_dtypes.bfloat16

E = 2048
T = 8192
H = 16
D = 128
MLP = 16384
NC = 8
EPS = 1e-5
TPC = T // NC
NG = 4
GT = T // NG
PC = GT // NC
NWIN = 1024
NWINS = MLP // NWIN
C_SCALE = float(np.float32(1.0) / np.sqrt(np.float32(128.0)))
R_E = float(np.float32(1.0 / E))

ENGS = ("sync", "vector", "scalar", "tensor", "gpsimd")
DEBUG = False



def _perm_enum(total):
    out = np.empty(total, dtype=np.int64)
    for c in range(total // 512):
        for i in range(4):
            for p in range(128):
                out[512 * c + 128 * i + p] = 512 * c + 4 * p + i
    return out


PI_E = _perm_enum(E)
PI_N = _perm_enum(MLP)
PI_D = np.concatenate([np.arange(0, 128, 2), np.arange(1, 128, 2)])


def _pack_tiles(mat, rt, ct):
    R, C = mat.shape
    return np.ascontiguousarray(
        mat.reshape(R // rt, rt, C // ct, ct).transpose(0, 2, 1, 3))


def _prep_packed(W_packed_int32, row_perm):
    P8 = np.asarray(W_packed_int32).astype(np.uint8)[row_perm]
    U = P8.view(np.uint16)
    Tt = np.ascontiguousarray(U.T)
    K4 = Tt.shape[0]
    return np.ascontiguousarray(Tt.reshape(K4 // 128, 128, -1))


def _hostprep(inp):
    d = {}
    x = np.asarray(inp["L0_attn_x"])[0]
    x0T = np.ascontiguousarray(x.T)[PI_E]
    own_cols = [np.concatenate([np.arange(GT * g + PC * r, GT * g + PC * (r + 1))
                                for g in range(NG)]) for r in range(NC)]
    d["own_cols"] = own_cols
    d["x0T_own"] = [np.ascontiguousarray(x0T[:, own_cols[r]]) for r in range(NC)]

    Wqkv = np.asarray(inp["L0_attn_W_qkv"])
    Wq = Wqkv[0:E].reshape(H, D, E)[:, PI_D].reshape(E, E)
    Wk = Wqkv[E:2 * E].reshape(H, D, E)[:, PI_D].reshape(E, E)
    Wv = Wqkv[2 * E:3 * E]
    d["wqkvT"] = []
    for r in range(NC):
        Wf = np.concatenate([W[D * 2 * r: D * (2 * r + 2)] for W in (Wq, Wk, Wv)], 0)
        WfT = np.ascontiguousarray(Wf[:, PI_E].T)
        tl = _pack_tiles(WfT, 128, 128)
        d["wqkvT"].append(np.ascontiguousarray(
            tl.transpose(0, 2, 1, 3).reshape(16, 128, 6 * 128)))

    Wp = np.asarray(inp["L0_attn_W_proj"])[PI_E]
    wpT = _pack_tiles(np.ascontiguousarray(Wp.T), 128, 128)
    d["wpT_own"] = [np.ascontiguousarray(
        wpT[2 * r:2 * r + 2].transpose(0, 2, 1, 3).reshape(2, 128, 16 * 128))
        for r in range(NC)]

    d["pk_g"] = _prep_packed(inp["L0_ffn_gate_W_packed"], PI_N)
    d["pk_u"] = _prep_packed(inp["L0_ffn_up_W_packed"], PI_N)
    d["pk_d"] = _prep_packed(inp["L0_ffn_down_W_packed"], PI_E)
    d["scT_g"] = np.ascontiguousarray(np.asarray(inp["L0_ffn_gate_scales"])[PI_N].T)
    d["scT_u"] = np.ascontiguousarray(np.asarray(inp["L0_ffn_up_scales"])[PI_N].T)
    d["scT_d"] = np.ascontiguousarray(np.asarray(inp["L0_ffn_down_scales"])[PI_E].T)

    inv = 1.0 / (10000.0 ** (np.arange(0, D, 2, dtype=np.float64) / D))
    ang = np.outer(inv, np.arange(T, dtype=np.float64))
    cos = np.cos(ang).astype(np.float32)
    sin = np.sin(ang).astype(np.float32)
    d["CS1"] = np.ascontiguousarray(np.concatenate([cos, cos], 0))
    d["CS2"] = np.ascontiguousarray(np.concatenate([-sin, sin], 0))

    cgrid = np.arange(896)[None, :]
    igrid = np.arange(128)[:, None]
    d["maskM"] = (cgrid >= igrid + 384).astype(NPBF16)
    d["ident"] = np.eye(128, dtype=NPBF16)
    return d



class KB:
    def __init__(self, nc, es):
        self.nc = nc
        self.es = es
        self.prog = {e: [] for e in ENGS}
        self.sems = {}
        self.cnt = {}

    def sem(self, *names):
        for n in names:
            self.sems[n] = self.es.enter_context(self.nc.semaphore(n))
            self.cnt[n] = 0

    def step(self, eng, fn, sem=None, k=1):
        if sem is not None:
            self.cnt[sem] += k
            tgt = self.cnt[sem]
            s = self.sems[sem]

            def go(e, fn=fn, s=s, k=k):
                fn(e).then_inc(s, k)
            self.prog[eng].append(go)
            return tgt
        self.prog[eng].append(lambda e, fn=fn: fn(e))
        return None

    def dma(self, eng, out, in_, sem):
        return self.step(eng, lambda e, o=out, i=in_: e.dma_start(out=o, in_=i),
                         sem=sem, k=16)

    def wait(self, eng, sem, val=None):
        v = self.cnt[sem] if val is None else val
        if v <= 0:
            return
        s = self.sems[sem]
        self.prog[eng].append(lambda e, s=s, v=v: e.wait_ge(s, v))

    def barrier(self):
        for e in ENGS:
            self.step(e, lambda eng: eng.nop(), sem="b_" + e, k=1)
        snap = {e: self.cnt["b_" + e] for e in ENGS}
        for e in ENGS:
            for o in ENGS:
                if o != e:
                    self.wait(e, "b_" + o, snap[o])



def build_program():
    nc = bass.Bass("TRN2", target_bir_lowering=False)
    es = ExitStack()
    kb = KB(nc, es)
    kb.sem("qL", "qM", "qG", "qH", "qC", "qD", "qE", "qF", "sV", "sS", "sT", "sG", "sCC", *["b_" + e for e in ENGS])

    def sync_fence():
        kb.wait("sync", "sT")
        kb.wait("sync", "sV")
        kb.wait("sync", "qG")
        kb.wait("sync", "qH")

    din = lambda name, shape, dtype: nc.dram_tensor(
        name, shape, dtype, kind="ExternalInput").ap()
    x0_d = din("x0T_own", [E, TPC], F32)
    wqkv_d = din("wqkvT", [16, 128, 6 * 128], F32)
    wp_d = din("wpT_own", [2, 128, 16 * 128], F32)
    pkg_d = din("pk_g", [4, 128, MLP], U16)
    pku_d = din("pk_u", [4, 128, MLP], U16)
    pkd_d = din("pk_d", [32, 128, E], U16)
    scg_d = din("scT_g", [64, MLP], F32)
    scu_d = din("scT_u", [64, MLP], F32)
    scd_d = din("scT_d", [512, E], F32)
    cs1_d = din("CS1", [128, T], F32)
    cs2_d = din("CS2", [128, T], F32)
    mask_d = din("maskM", [128, 896], BF16)
    ident_d = din("ident", [128, 128], BF16)

    out_d = nc.dram_tensor("out", [E, TPC], F32, kind="ExternalOutput").ap()

    ikind = "ExternalOutput" if DEBUG else "Internal"
    cc_b0_in = [nc.dram_tensor(f"cc_b0_in{h}", [E, 512], BF16,
                               kind="Internal").ap() for h in range(2)]
    cc_b0_out = [nc.dram_tensor(f"cc_b0_out{h}", [NC * E, 512], BF16,
                                addr_space="Shared", kind="Internal").ap()
                 for h in range(2)]
    rs_in = [nc.dram_tensor(f"rs_in{g}", [NC * E, PC], F32, kind="Internal").ap()
             for g in range(NG)]
    rs_out = [nc.dram_tensor(f"rs_out{g}", [E, PC], F32, kind="Internal").ap()
              for g in range(NG)]
    x0b_sp = nc.dram_tensor("x0b_sp", [E, TPC], BF16, kind=ikind).ap()
    acc_dram = nc.dram_tensor("acc_dram", [E, TPC], F32, kind="Internal").ap()
    x1b_sp = nc.dram_tensor("x1b_sp", [E, TPC], BF16, kind=ikind).ap()

    ones128 = es.enter_context(nc.sbuf_tensor("ones128", [128, 1], BF16))
    ones1 = es.enter_context(nc.sbuf_tensor("ones1", [1, 128], BF16))
    ones1f = es.enter_context(nc.sbuf_tensor("ones1f", [1, 128], F32))
    rw_m = es.enter_context(nc.sbuf_tensor("rw_m", [1, 512], F32))
    rw_v = es.enter_context(nc.sbuf_tensor("rw_v", [1, 512], F32))
    rw_t = es.enter_context(nc.sbuf_tensor("rw_t", [1, 512], F32))
    rw_mb = es.enter_context(nc.sbuf_tensor("rw_mb", [1, 512], BF16))
    rw_rb = es.enter_context(nc.sbuf_tensor("rw_rb", [1, 512], BF16))
    epsb = es.enter_context(nc.sbuf_tensor("epsb", [1, 1], F32))

    kb.step("vector", lambda v: v.memset(ones128[:], 1.0))
    kb.step("vector", lambda v: v.memset(ones1[:], 1.0))
    kb.step("vector", lambda v: v.memset(ones1f[:], 1.0))
    kb.step("vector", lambda v: v.memset(epsb[:], EPS), sem="sV")

    def ln_stats(strips, tw, ps_s, ps_s2, ps_bc, sqst, mB, rB, f32out=False):
        nstr = len(strips)
        kb.wait("tensor", "sV")
        kb.wait("scalar", "sV")
        s2_tgt = {}
        for j, xs in enumerate(strips):
            kb.step("tensor", lambda t, xs=xs, j=j: t.matmul(
                out=ps_s[0:1, 0:tw], lhsT=ones128[:], rhs=xs,
                start=(j == 0), stop=(j == nstr - 1)), sem="sT")
            slot = (j % 4) * tw
            if j >= 4:
                kb.wait("scalar", "sT", s2_tgt[j - 4])
            kb.step("scalar", lambda s, xs=xs, slot=slot: s.activation(
                out=sqst[:, slot:slot + tw], in_=xs, func=AF.Square), sem="sS")
            kb.wait("tensor", "sS")
            s2_tgt[j] = kb.step("tensor", lambda t, slot=slot, j=j: t.matmul(
                out=ps_s2[0:1, 0:tw], lhsT=ones128[:], rhs=sqst[:, slot:slot + tw],
                start=(j == 0), stop=(j == nstr - 1)), sem="sT")
        kb.wait("vector", "sT")
        kb.step("vector", lambda v: v.tensor_scalar_mul(
            rw_m[0:1, 0:tw], ps_s[0:1, 0:tw], R_E))
        kb.step("vector", lambda v: v.tensor_tensor(
            out=rw_t[0:1, 0:tw], in0=rw_m[0:1, 0:tw], in1=rw_m[0:1, 0:tw],
            op=ALU.mult))
        kb.step("vector", lambda v: v.scalar_tensor_tensor(
            out=rw_v[0:1, 0:tw], in0=ps_s2[0:1, 0:tw], scalar=R_E,
            in1=rw_t[0:1, 0:tw], op0=ALU.mult, op1=ALU.subtract), sem="sV")
        kb.wait("scalar", "sV")
        kb.step("scalar", lambda s: s.activation(
            out=rw_v[0:1, 0:tw], in_=rw_v[0:1, 0:tw], func=AF.Sqrt, bias=epsb[0:1, 0:1]),
            sem="sS")
        kb.wait("vector", "sS")
        kb.step("vector", lambda v: v.reciprocal(
            out=rw_v[0:1, 0:tw], in_=rw_v[0:1, 0:tw]))
        if f32out:
            kb.step("vector", lambda v: v.engine_nop(), sem="sV")
            m_rhs, r_rhs, l_ones = rw_m, rw_v, ones1f
        else:
            kb.step("vector", lambda v: v.tensor_copy(
                out=rw_mb[0:1, 0:tw], in_=rw_m[0:1, 0:tw]))
            kb.step("vector", lambda v: v.tensor_copy(
                out=rw_rb[0:1, 0:tw], in_=rw_v[0:1, 0:tw]), sem="sV")
            m_rhs, r_rhs, l_ones = rw_mb, rw_rb, ones1
        kb.wait("tensor", "sV")
        kb.step("tensor", lambda t: t.matmul(
            out=ps_bc[:, 0:tw], lhsT=l_ones[:], rhs=m_rhs[0:1, 0:tw],
            start=True, stop=True), sem="sT")
        kb.wait("vector", "sT")
        kb.step("vector", lambda v: v.tensor_copy(
            out=mB[:, 0:tw], in_=ps_bc[:, 0:tw]), sem="sV")
        kb.wait("tensor", "sV")
        kb.step("tensor", lambda t: t.matmul(
            out=ps_bc[:, 0:tw], lhsT=l_ones[:], rhs=r_rhs[0:1, 0:tw],
            start=True, stop=True), sem="sT")
        kb.wait("vector", "sT")
        kb.step("vector", lambda v: v.tensor_copy(
            out=rB[:, 0:tw], in_=ps_bc[:, 0:tw]), sem="sV")

    es_w = ExitStack()
    wqkvb = es_w.enter_context(nc.sbuf_tensor("wqkvb", [128, 6 * 16 * 128], BF16))
    wpb = es_w.enter_context(nc.sbuf_tensor("wpb", [128, 2 * 16 * 128], BF16))
    maskM = es_w.enter_context(nc.sbuf_tensor("maskM_s", [128, 896], BF16))
    identb = es_w.enter_context(nc.sbuf_tensor("identb", [128, 128], BF16))

    kb.dma("sync", identb[:], ident_d, "qC")
    kb.dma("sync", maskM[:], mask_d, "qC")
    kb.wait("tensor", "qC")
    kb.wait("vector", "qC")

    es0 = ExitStack()
    p0f = es0.enter_context(nc.sbuf_tensor("p0f", [128, 2 * 2048], F32))
    x0bs = es0.enter_context(nc.sbuf_tensor("x0bs", [128, 16 * TPC], BF16))
    sqst0 = es0.enter_context(nc.sbuf_tensor("sqst0", [128, 2048], BF16))
    nrm0 = es0.enter_context(nc.sbuf_tensor("nrm0", [128, 2 * 512], BF16))
    mB0 = es0.enter_context(nc.sbuf_tensor("mB0", [128, 512], BF16))
    rB0 = es0.enter_context(nc.sbuf_tensor("rB0", [128, 512], BF16))
    ps_s = es0.enter_context(nc.psum_tensor("ps_s", [1, 512], F32))
    ps_s2 = es0.enter_context(nc.psum_tensor("ps_s2", [1, 512], F32))
    ps_bc = es0.enter_context(nc.psum_tensor("ps_bc", [128, 512], F32))

    tile_jobs = []
    for ec in range(16):
        dst = wqkvb[:].rearrange("p (f e c) -> p f e c", f=6, e=16)[:, :, ec, :]
        tile_jobs.append((wqkv_d[ec], dst, 768))
    for e2 in range(2):
        dst = wpb[:, e2 * 2048:(e2 + 1) * 2048].rearrange(
            "p (f c) -> p f c", f=16)
        tile_jobs.append((wp_d[e2], dst, 2048))
    for j, (srcap, dst, w) in enumerate(tile_jobs):
        par = j % 2
        ldsem = "qL" if par == 0 else "qM"
        if j >= 2:
            kb.wait("sync", "sV")
        kb.dma("sync", p0f[:, 2048 * par:2048 * par + w], srcap, ldsem)
        kb.wait("vector", ldsem)
        kb.step("vector", lambda v, par=par, dst=dst, w=w: v.tensor_copy(
            out=dst, in_=p0f[:, 2048 * par:2048 * par + w].rearrange(
                "p (a c) -> p a c", c=128)), sem="sV")


    for ec in range(16):
        par = ec % 2
        ldsem = "qL" if par == 0 else "qM"
        kb.wait("sync", "sV")
        kb.dma("sync", p0f[:, 2048 * par:2048 * par + 1024],
               x0_d[128 * ec:128 * (ec + 1), :], ldsem)
        kb.wait("vector", ldsem)
        kb.step("vector", lambda v, ec=ec, par=par: v.tensor_copy(
            out=x0bs[:, TPC * ec:TPC * (ec + 1)],
            in_=p0f[:, 2048 * par:2048 * par + 1024]), sem="sV")
    kb.wait("gpsimd", "sV")
    for ec in range(16):
        kb.dma("gpsimd", x0b_sp[128 * ec:128 * (ec + 1), :],
               x0bs[:, TPC * ec:TPC * (ec + 1)], "qG")

    ag_tgt = []
    for tb in range(2):
        strips = [x0bs[:, TPC * ec + 512 * tb: TPC * ec + 512 * (tb + 1)]
                  for ec in range(16)]
        ln_stats(strips, 512, ps_s, ps_s2, ps_bc, sqst0, mB0, rB0)
        for ec in range(16):
            par = ec % 2
            src = strips[ec]
            stsem = "qG" if par == 0 else "qH"
            kb.wait("vector", stsem)
            kb.step("vector", lambda v, src=src, par=par: v.tensor_tensor(
                out=nrm0[:, 512 * par:512 * (par + 1)], in0=src, in1=mB0[:],
                op=ALU.subtract))
            kb.step("vector", lambda v, par=par: v.tensor_tensor(
                out=nrm0[:, 512 * par:512 * (par + 1)],
                in0=nrm0[:, 512 * par:512 * (par + 1)], in1=rB0[:],
                op=ALU.mult), sem="sV")
            kb.wait("gpsimd", "sV")
            kb.dma("gpsimd",
                   cc_b0_in[tb][128 * ec:128 * (ec + 1), :],
                   nrm0[:, 512 * par:512 * (par + 1)], stsem)
        kb.wait("gpsimd", "qG")
        kb.wait("gpsimd", "qH")
        kb.step("gpsimd", lambda g, tb=tb: g.collective_compute(
            "AllGather", ALU.bypass, ins=[cc_b0_in[tb]], outs=[cc_b0_out[tb]],
            replica_groups=[list(range(NC))]), sem="sCC")
        ag_tgt.append(kb.cnt["sCC"])

    sync_fence()
    es0.close()

    es12 = ExitStack()
    Kt = es12.enter_context(nc.sbuf_tensor("Kt", [128, 2 * T], BF16))
    Vt = es12.enter_context(nc.sbuf_tensor("Vt", [128, 2 * T], BF16))
    Qt = es12.enter_context(nc.sbuf_tensor("Qt", [128, 2 * T], BF16))

    es1 = ExitStack()
    b0c = es1.enter_context(nc.sbuf_tensor("b0c", [128, 2 * 16 * 512], BF16))
    zb = es1.enter_context(nc.sbuf_tensor("zb", [128, 6 * 512], BF16))
    rt1 = es1.enter_context(nc.sbuf_tensor("rt1", [128, 512], BF16))
    rt2 = es1.enter_context(nc.sbuf_tensor("rt2", [128, 512], BF16))
    csb = es1.enter_context(nc.sbuf_tensor("csb", [128, 2 * 1024], BF16))
    csf = es1.enter_context(nc.sbuf_tensor("csf", [128, 2 * 1024], F32))
    ps_qkv = [es1.enter_context(nc.psum_tensor(f"ps_qkv{f}", [128, 512], F32))
              for f in range(6)]
    ps_vt = es1.enter_context(nc.psum_tensor("ps_vt", [128, 512], BF16))


    evac_done = 0
    evac_f = {}
    b0_loaded = {}
    mm_done = {}
    cs_cast = {}
    for tc in range(16 + 1):
        par = tc % 2
        if tc < 16:
            ldsem = "qL" if par == 0 else "qM"
            if tc >= 2:
                kb.wait("sync", "sT", mm_done[tc - 2])
            r0 = 2 * (tc % 4)
            half = tc // 8
            lc = 256 * (tc // 4) - 512 * half
            kb.wait("sync", "sCC", ag_tgt[half])
            for ec in range(16):
                for rr in range(2):
                    kb.dma("sync",
                           b0c[:, (par * 16 + ec) * 512 + 256 * rr:
                               (par * 16 + ec) * 512 + 256 * (rr + 1)],
                           cc_b0_out[half][E * (r0 + rr) + 128 * ec:
                                           E * (r0 + rr) + 128 * (ec + 1),
                                           lc:lc + 256],
                           ldsem)
            if tc >= 2:
                kb.wait("sync", "sV", cs_cast.get(tc - 2, 0))
            kb.dma("sync", csf[:, 1024 * par:1024 * par + 512],
                   cs1_d[:, 512 * tc:512 * (tc + 1)], ldsem)
            kb.dma("sync", csf[:, 1024 * par + 512:1024 * (par + 1)],
                   cs2_d[:, 512 * tc:512 * (tc + 1)], ldsem)
            b0_loaded[tc] = (ldsem, kb.cnt[ldsem])
            kb.wait("vector", *b0_loaded[tc])
            kb.step("vector", lambda v, par=par: v.tensor_copy(
                out=csb[:, 1024 * par:1024 * (par + 1)],
                in_=csf[:, 1024 * par:1024 * (par + 1)]), sem="sV")
            cs_cast[tc] = kb.cnt["sV"]
        if tc >= 1:
            tcc = tc - 1
            parc = tcc % 2
            kb.wait("tensor", *b0_loaded[tcc])
            for f in range(6):
                if tcc >= 1:
                    kb.wait("tensor", "sV", evac_f[(tcc - 1, f)])
                for ec in range(16):
                    w_ap = wqkvb[:, (f * 16 + ec) * 128:(f * 16 + ec + 1) * 128]
                    b_ap = b0c[:, (parc * 16 + ec) * 512:(parc * 16 + ec + 1) * 512]
                    kb.step("tensor", lambda t, w=w_ap, b=b_ap, f=f, ec=ec: t.matmul(
                        out=ps_qkv[f][:], lhsT=w, rhs=b,
                        start=(ec == 0), stop=(ec == 15)),
                        sem="sT" if ec == 15 else None)
            mm_done[tcc] = kb.cnt["sT"]
            cs1sl = csb[:, 1024 * parc:1024 * parc + 512]
            cs2sl = csb[:, 1024 * parc + 512:1024 * (parc + 1)]
            mmb = mm_done[tcc] - 6
            for f in range(4):
                hh = f % 2
                dst = Qt if f < 2 else Kt
                dsl = slice(T * hh + 512 * tcc, T * hh + 512 * (tcc + 1))
                zsl = zb[:, 512 * f:512 * (f + 1)]
                kb.wait("vector", "sT", mmb + f + 1)
                evac_f[(tcc, f)] = kb.step(
                    "vector", lambda v, f=f, zsl=zsl: v.tensor_copy(
                        out=zsl, in_=ps_qkv[f][:]), sem="sV")
                kb.step("vector", lambda v, zsl=zsl, cs1sl=cs1sl: v.tensor_tensor(
                    out=rt1[:], in0=zsl, in1=cs1sl, op=ALU.mult))
                kb.step("vector", lambda v, zsl=zsl, cs2sl=cs2sl: v.tensor_tensor(
                    out=rt2[0:64, :], in0=zsl[64:128, :], in1=cs2sl[64:128, :],
                    op=ALU.mult))
                kb.step("vector", lambda v, zsl=zsl, cs2sl=cs2sl: v.tensor_tensor(
                    out=rt2[64:128, :], in0=zsl[0:64, :], in1=cs2sl[0:64, :],
                    op=ALU.mult))
                kb.step("vector", lambda v, dst=dst, dsl=dsl: v.tensor_tensor(
                    out=dst[:, dsl], in0=rt1[:], in1=rt2[:], op=ALU.subtract),
                    sem="sV")
            for f in (4, 5):
                hh = f % 2
                dsl = slice(T * hh + 512 * tcc, T * hh + 512 * (tcc + 1))
                zsl = zb[:, 512 * f:512 * (f + 1)]
                kb.wait("vector", "sT", mmb + f + 1)
                evac_f[(tcc, f)] = kb.step(
                    "vector", lambda v, f=f, zsl=zsl: v.tensor_copy(
                        out=zsl, in_=ps_qkv[f][:]), sem="sV")
                kb.wait("tensor", "sV")
                for j in range(4):
                    kb.step("tensor", lambda t, zsl=zsl, j=j: t.transpose(
                        out=ps_vt[:, 128 * j:128 * (j + 1)],
                        in_=zsl[:, 128 * j:128 * (j + 1)], identity=identb[:]),
                        sem="sT" if j == 3 else None)
                kb.wait("vector", "sT")
                kb.step("vector", lambda v, dsl=dsl: v.tensor_copy(
                    out=Vt[:, dsl], in_=ps_vt[:]), sem="sV")
            evac_done = kb.cnt["sV"]

    kb.barrier()
    es1.close()

    if DEBUG:
        kqv_dbg = {nm: nc.dram_tensor(nm + "_dbg", [128, 2 * T], BF16,
                                      kind="ExternalOutput").ap()
                   for nm in ("Kt", "Qt", "Vt")}
        for nm, tile in (("Kt", Kt), ("Qt", Qt), ("Vt", Vt)):
            kb.dma("gpsimd", kqv_dbg[nm], tile[:], "qG")
        b0_dbg = nc.dram_tensor("b0_dbg", [2 * NC * E, 512], BF16,
                                kind="ExternalOutput").ap()
        for h in range(2):
            kb.dma("gpsimd", b0_dbg[NC * E * h:NC * E * (h + 1), :],
                   cc_b0_out[h], "qG")
        wq_dbg = nc.dram_tensor("wq_dbg", [128, 6 * 16 * 128], BF16,
                                kind="ExternalOutput").ap()
        kb.dma("gpsimd", wq_dbg, wqkvb[:], "qG")
        kb.wait("gpsimd", "qG")
        kb.barrier()

    es2p = ExitStack()
    Pt = es2p.enter_context(nc.sbuf_tensor("Pt", [128, 2 * 1024], BF16))
    Ygrp = es2p.enter_context(nc.sbuf_tensor("Ygrp", [128, 2 * GT], BF16))
    rlB = es2p.enter_context(nc.sbuf_tensor("rlB", [128, 1024], BF16))
    oev = es2p.enter_context(nc.sbuf_tensor("oev", [128, 2 * 512], F32))

    exp_done = {}
    prev_proj_sv = 0
    pend_tail = None
    st_yh = {"t": 0}
    for g in range(NG):
        esg = ExitStack()
        ps_sc = [esg.enter_context(nc.psum_tensor(f"ps_sc{g}_{j}", [128, 1024], F32))
                 for j in range(2)]
        ps_y = [esg.enter_context(nc.psum_tensor(f"ps_y{g}_{j}", [128, 512], F32))
                for j in range(2)]
        ps_l = [esg.enter_context(nc.psum_tensor(f"ps_l{g}_{j}", [128, 512], F32))
                for j in range(2)]
        kb.wait("tensor", "sV", prev_proj_sv)
        for hh in range(2):
            for qcp_in_g in range(2):
                qcp = 2 * g + qcp_in_g
                qc0, qc1 = 2 * qcp, 2 * qcp + 1
                kmax0, kmax1 = 4 * qc0 + 3, 4 * qc1 + 3
                need_v = {}
                for kt in range(kmax1 + 2):
                    if kt <= kmax1:
                        par = kt % 2
                        both = kt <= kmax0
                        prev = exp_done.get((hh, qcp, kt - 2))
                        if prev is None:
                            prev = kb.cnt["sS"] if kt < 2 else None
                        if prev:
                            kb.wait("tensor", "sS", prev)
                        q_ap1 = Qt[:, T * hh + 512 * qc1: T * hh + 512 * (qc1 + 1)]
                        k_ap = Kt[:, T * hh + 128 * kt: T * hh + 128 * (kt + 1)]
                        if both:
                            q_ap0 = Qt[:, T * hh + 512 * qc0: T * hh + 512 * (qc0 + 1)]
                            kb.step("tensor", lambda t, k=k_ap, q=q_ap0, par=par:
                                    t.matmul(out=ps_sc[par][:, 0:512], lhsT=k, rhs=q,
                                             start=True, stop=True))
                        kb.step("tensor", lambda t, k=k_ap, q=q_ap1, par=par:
                                t.matmul(out=ps_sc[par][:, 512:1024], lhsT=k, rhs=q,
                                         start=True, stop=True), sem="sT")
                        kb.wait("scalar", "sT")
                        if both:
                            kb.step("scalar", lambda s, par=par: s.activation(
                                out=Pt[:, 1024 * par:1024 * (par + 1)],
                                in_=ps_sc[par][:], func=AF.Exp, scale=C_SCALE),
                                sem="sS")
                        else:
                            kb.step("scalar", lambda s, par=par: s.activation(
                                out=Pt[:, 1024 * par + 512:1024 * (par + 1)],
                                in_=ps_sc[par][:, 512:1024], func=AF.Exp,
                                scale=C_SCALE), sem="sS")
                        exp_done[(hh, qcp, kt)] = kb.cnt["sS"]
                        nv = 0
                        if both and 4 * qc0 <= kt <= 4 * qc0 + 3:
                            off = 384 - 128 * (kt - 4 * qc0)
                            kb.wait("vector", "sS")
                            kb.step("vector", lambda v, par=par, off=off:
                                    v.tensor_tensor(
                                        out=Pt[:, 1024 * par:1024 * par + 512],
                                        in0=Pt[:, 1024 * par:1024 * par + 512],
                                        in1=maskM[:, off:off + 512], op=ALU.mult),
                                    sem="sV")
                            nv = kb.cnt["sV"]
                        if 4 * qc1 <= kt <= 4 * qc1 + 3:
                            off = 384 - 128 * (kt - 4 * qc1)
                            kb.wait("vector", "sS")
                            kb.step("vector", lambda v, par=par, off=off:
                                    v.tensor_tensor(
                                        out=Pt[:, 1024 * par + 512:1024 * (par + 1)],
                                        in0=Pt[:, 1024 * par + 512:1024 * (par + 1)],
                                        in1=maskM[:, off:off + 512], op=ALU.mult),
                                    sem="sV")
                            nv = kb.cnt["sV"]
                        need_v[kt] = nv
                    if kt == 1 and pend_tail is not None:
                        pend_tail()
                        pend_tail = None
                    if kt >= 1:
                        ky = kt - 1
                        pary = ky % 2
                        bothy = ky <= kmax0
                        if ky == 0 and st_yh["t"]:
                            kb.wait("tensor", "sV", st_yh["t"])
                        kb.wait("tensor", "sS", exp_done[(hh, qcp, ky)])
                        if need_v[ky]:
                            kb.wait("tensor", "sV", need_v[ky])
                        v_ap = Vt[:, T * hh + 128 * ky: T * hh + 128 * (ky + 1)]
                        p0_ap = Pt[:, 1024 * pary:1024 * pary + 512]
                        p1_ap = Pt[:, 1024 * pary + 512:1024 * (pary + 1)]
                        if bothy:
                            kb.step("tensor", lambda t, v=v_ap, p=p0_ap, ky=ky,
                                    k0=kmax0: t.matmul(out=ps_y[0][:], lhsT=v,
                                                       rhs=p, start=(ky == 0),
                                                       stop=(ky == k0)))
                            kb.step("tensor", lambda t, p=p0_ap, ky=ky, k0=kmax0:
                                    t.matmul(out=ps_l[0][0:1, :], lhsT=ones128[:],
                                             rhs=p, start=(ky == 0),
                                             stop=(ky == k0)))
                        kb.step("tensor", lambda t, v=v_ap, p=p1_ap, ky=ky,
                                k1=kmax1: t.matmul(out=ps_y[1][:], lhsT=v, rhs=p,
                                                   start=(ky == 0),
                                                   stop=(ky == k1)))
                        kb.step("tensor", lambda t, p=p1_ap, ky=ky, k1=kmax1:
                                t.matmul(out=ps_l[1][0:1, :], lhsT=ones128[:],
                                         rhs=p, start=(ky == 0), stop=(ky == k1)),
                                sem="sT" if ky == kmax1 else None)
                kb.wait("vector", "sT")
                kb.step("vector", lambda v: v.reciprocal(
                    out=rw_m[0:1, 0:512], in_=ps_l[0][0:1, :]))
                kb.step("vector", lambda v: v.reciprocal(
                    out=rw_v[0:1, 0:512], in_=ps_l[1][0:1, :]))
                kb.step("vector", lambda v: v.tensor_copy(
                    out=rw_mb[0:1, 0:512], in_=rw_m[0:1, 0:512]))
                kb.step("vector", lambda v: v.tensor_copy(
                    out=rw_rb[0:1, 0:512], in_=rw_v[0:1, 0:512]), sem="sV")
                vA = kb.cnt["sV"]

                def mk_tail(hh=hh, qc0=qc0, qc1=qc1, vA=vA, g=g,
                            ps_l=ps_l, ps_y=ps_y, Ygrp=Ygrp):
                    def tail():
                        kb.wait("tensor", "sV", vA)
                        kb.step("tensor", lambda t: t.matmul(
                            out=ps_l[0][:], lhsT=ones1[:],
                            rhs=rw_mb[0:1, 0:512], start=True, stop=True))
                        kb.step("tensor", lambda t: t.matmul(
                            out=ps_l[1][:], lhsT=ones1[:],
                            rhs=rw_rb[0:1, 0:512], start=True, stop=True),
                            sem="sT")
                        bc = kb.cnt["sT"]
                        kb.wait("vector", "sT", bc)
                        for j in range(2):
                            kb.step("vector", lambda v, j=j: v.tensor_copy(
                                out=rlB[:, 512 * j:512 * (j + 1)],
                                in_=ps_l[j][:]))
                        for j, qc in ((0, qc0), (1, qc1)):
                            ysl = Ygrp[:, GT * hh + 512 * (qc - 4 * g):
                                       GT * hh + 512 * (qc - 4 * g + 1)]
                            kb.step("vector", lambda v, j=j, ysl=ysl:
                                    v.tensor_tensor(out=ysl, in0=ps_y[j][:],
                                                    in1=rlB[:, 512 * j:
                                                            512 * (j + 1)],
                                                    op=ALU.mult), sem="sV")
                        st_yh["t"] = kb.cnt["sV"]
                    return tail
                pend_tail = mk_tail()
        pend_tail()
        pend_tail = None
        esg.close()
        esp = ExitStack()
        ps_o = [esp.enter_context(nc.psum_tensor(f"ps_o{g}_{j}", [128, 512], F32))
                for j in range(2)]
        proj_sv_base = kb.cnt["sV"]
        kb.wait("tensor", "sV")
        kb.wait("vector", "qG")
        kb.wait("vector", "qH")
        nproj = 0
        for tch in range(4):
            for fc in range(16):
                j = nproj % 2
                if nproj >= 2:
                    kb.wait("tensor", "sV", proj_sv_base + nproj - 1)
                for e2 in range(2):
                    w_ap = wpb[:, (e2 * 16 + fc) * 128:(e2 * 16 + fc + 1) * 128]
                    y_ap = Ygrp[:, GT * e2 + 512 * tch: GT * e2 + 512 * (tch + 1)]
                    kb.step("tensor", lambda t, w=w_ap, y=y_ap, j=j, e2=e2:
                            t.matmul(out=ps_o[j][:], lhsT=w, rhs=y,
                                     start=(e2 == 0), stop=(e2 == 1)),
                            sem="sT" if e2 == 1 else None)
                kb.wait("vector", "sT")
                stsem = "qG" if j == 0 else "qH"
                if nproj >= 2:
                    kb.wait("vector", stsem)
                kb.step("vector", lambda v, j=j: v.tensor_copy(
                    out=oev[:, 512 * j:512 * (j + 1)], in_=ps_o[j][:]), sem="sV")
                kb.wait("gpsimd", "sV")
                for rr in range(2):
                    p = 2 * tch + rr
                    kb.dma("gpsimd",
                           rs_in[g][E * p + 128 * fc: E * p + 128 * (fc + 1), :],
                           oev[:, 512 * j + 256 * rr: 512 * j + 256 * (rr + 1)],
                           stsem)
                nproj += 1
        prev_proj_sv = kb.cnt["sV"]
        kb.wait("gpsimd", "qG")
        kb.wait("gpsimd", "qH")
        if DEBUG:
            rsin_dbg = nc.dram_tensor(f"rsin_dbg{g}", [NC * E, PC], F32,
                                      kind="ExternalOutput").ap()
            kb.dma("gpsimd", rsin_dbg, rs_in[g], "qG")
            kb.wait("gpsimd", "qG")
        kb.step("gpsimd", lambda gg, g=g: gg.collective_compute(
            "ReduceScatter", ALU.add, ins=[rs_in[g]], outs=[rs_out[g]],
            replica_groups=[list(range(NC))]), sem="sCC")
        esp.close()

    sync_fence()
    es2p.close()
    es12.close()
    es_w.close()

    es34 = ExitStack()
    b_ffn = es34.enter_context(nc.sbuf_tensor("b_ffn", [128, 16 * TPC], BF16))

    es3 = ExitStack()
    x0r = es3.enter_context(nc.sbuf_tensor("x0r", [128, 16 * TPC], BF16))
    x1bs = es3.enter_context(nc.sbuf_tensor("x1bs", [128, 16 * TPC], BF16))
    oBs = es3.enter_context(nc.sbuf_tensor("oBs", [128, 16 * 256], BF16))
    oF = es3.enter_context(nc.sbuf_tensor("oF", [128, 2 * 4096], F32))
    sqst3 = es3.enter_context(nc.sbuf_tensor("sqst3", [128, 2048], BF16))
    mB3 = es3.enter_context(nc.sbuf_tensor("mB3", [128, 512], BF16))
    rB3 = es3.enter_context(nc.sbuf_tensor("rB3", [128, 512], BF16))
    ps_s3 = es3.enter_context(nc.psum_tensor("ps_s3", [1, 512], F32))
    ps_s23 = es3.enter_context(nc.psum_tensor("ps_s23", [1, 512], F32))
    ps_bc3 = es3.enter_context(nc.psum_tensor("ps_bc3", [128, 512], F32))

    for ec in range(16):
        kb.dma("sync", x0r[:, TPC * ec:TPC * (ec + 1)],
               x0b_sp[128 * ec:128 * (ec + 1), :], "qC")
    sCC_rs0 = kb.cnt["sCC"] - NG
    if DEBUG:
        rs_dbg = nc.dram_tensor("rs_dbg", [E, PC], F32, kind="ExternalOutput").ap()
        kb.wait("gpsimd", "sCC", sCC_rs0 + 1)
        kb.dma("gpsimd", rs_dbg, rs_out[0], "qG")
    cast_done = {}
    for g in range(NG):
        slot = g % 2
        ldsem3 = "qD" if slot == 0 else "qE"
        kb.wait("sync", "sCC", sCC_rs0 + 1 + g)
        if g >= 2:
            kb.wait("sync", "sV", cast_done[g - 2])
        for ec in range(16):
            kb.dma("sync", oF[:, 4096 * slot + 256 * ec:
                              4096 * slot + 256 * (ec + 1)],
                   rs_out[g][128 * ec:128 * (ec + 1), :], ldsem3)
        kb.wait("vector", ldsem3)
        for ec in range(16):
            kb.step("vector", lambda v, ec=ec, slot=slot: v.tensor_copy(
                out=oBs[:, 256 * ec:256 * (ec + 1)],
                in_=oF[:, 4096 * slot + 256 * ec:4096 * slot + 256 * (ec + 1)]),
                sem="sV")
        cast_done[g] = kb.cnt["sV"]
        strips = [oBs[:, 256 * ec:256 * (ec + 1)] for ec in range(16)]
        ln_stats(strips, 256, ps_s3, ps_s23, ps_bc3, sqst3, mB3, rB3)
        kb.wait("vector", "qC")
        for ec in range(16):
            osl = oBs[:, 256 * ec:256 * (ec + 1)]
            x0sl = x0r[:, TPC * ec + 256 * g: TPC * ec + 256 * (g + 1)]
            x1sl = x1bs[:, TPC * ec + 256 * g: TPC * ec + 256 * (g + 1)]
            kb.step("vector", lambda v, osl=osl: v.tensor_tensor(
                out=osl, in0=osl, in1=mB3[:, 0:256], op=ALU.subtract))
            kb.step("vector", lambda v, osl=osl: v.tensor_tensor(
                out=osl, in0=osl, in1=rB3[:, 0:256], op=ALU.mult))
            kb.step("vector", lambda v, osl=osl, x0sl=x0sl, x1sl=x1sl:
                    v.tensor_tensor(out=x1sl, in0=osl, in1=x0sl, op=ALU.add),
                    sem="sV")

    kb.wait("gpsimd", "sV")
    for ec in range(16):
        kb.dma("gpsimd", x1b_sp[128 * ec:128 * (ec + 1), :],
               x1bs[:, TPC * ec:TPC * (ec + 1)], "qG")
    for tb in range(2):
        strips = [x1bs[:, TPC * ec + 512 * tb: TPC * ec + 512 * (tb + 1)]
                  for ec in range(16)]
        ln_stats(strips, 512, ps_s3, ps_s23, ps_bc3, sqst3, mB3, rB3)
        for ec in range(16):
            src = strips[ec]
            dst = b_ffn[:, TPC * ec + 512 * tb: TPC * ec + 512 * (tb + 1)]
            kb.step("vector", lambda v, src=src, dst=dst: v.tensor_tensor(
                out=dst, in0=src, in1=mB3[:], op=ALU.subtract))
            kb.step("vector", lambda v, dst=dst: v.tensor_tensor(
                out=dst, in0=dst, in1=rB3[:], op=ALU.mult), sem="sV")

    sync_fence()
    es3.close()

    if DEBUG:
        bffn_dbg = nc.dram_tensor("bffn_dbg", [E, TPC], BF16,
                                  kind="ExternalOutput").ap()
        kb.wait("gpsimd", "sV")
        for ec in range(16):
            kb.dma("gpsimd", bffn_dbg[128 * ec:128 * (ec + 1), :],
                   b_ffn[:, TPC * ec:TPC * (ec + 1)], "qG")
        kb.wait("gpsimd", "qG")
        kb.barrier()

    FW = 512
    FWINS = MLP // FW
    es4 = ExitStack()
    gwin = es4.enter_context(nc.sbuf_tensor("gwin", [128, 8 * TPC], BF16))
    acc_sb = es4.enter_context(nc.sbuf_tensor("acc_sb", [128, 16 * 512], F32))
    es4a = ExitStack()
    w4buf = [es4a.enter_context(nc.sbuf_tensor(f"w4buf{s}", [128, 16384], U16))
             for s in range(2)]
    pkbuf = es4a.enter_context(nc.sbuf_tensor("pkbuf", [128, 2 * 4096], U16))
    s16buf = es4a.enter_context(nc.sbuf_tensor("s16buf", [128, 2 * 4096], BF16))
    scf = es4a.enter_context(nc.sbuf_tensor("scf", [128, 4096], F32))
    fstage = es4a.enter_context(nc.sbuf_tensor("fstage", [128, 2 * 512], F32))
    dq1 = es4a.enter_context(nc.sbuf_tensor("dq1", [128, 512], U16))
    dq2 = es4a.enter_context(nc.sbuf_tensor("dq2", [128, 512], U16))
    ps_gu = [es4a.enter_context(nc.psum_tensor(f"ps_gu{j}", [128, 512], F32))
             for j in range(4)]
    ps_dn = [es4a.enter_context(nc.psum_tensor(f"ps_dn{j}", [128, 512], F32))
             for j in range(4)]


    phases = []
    for w in range(FWINS):
        phases.append(("g", w))
        phases.append(("u", w))
        if w % 2 == 1:
            phases.append(("d", w // 2))

    def emit_phase_loads(kind, w, slot):
        ldsem = "qL" if slot == 0 else "qM"
        scsem = "qC" if slot == 0 else "qF"
        if kind in ("g", "u"):
            pk_ = pkg_d if kind == "g" else pku_d
            sc_ = scg_d if kind == "g" else scu_d
            for c in range(4):
                kb.dma("sync", pkbuf[:, 4096 * slot + 512 * c:
                                     4096 * slot + 512 * (c + 1)],
                       pk_[c, :, FW * w:FW * (w + 1)], ldsem)
                srcap = sc_[16 * c:16 * (c + 1), FW * w:FW * (w + 1)]
                srcap = srcap.unsqueeze(1).broadcast_to([16, 8, FW])
                kb.dma("sync", scf[:, 512 * c:512 * (c + 1)], srcap, scsem)
        else:
            for cc in range(2):
                c = 2 * w + cc
                kb.dma("sync", pkbuf[:, 4096 * slot + 2048 * cc:
                                     4096 * slot + 2048 * (cc + 1)],
                       pkd_d[c], ldsem)
                srcap = scd_d[16 * c:16 * (c + 1), :]
                srcap = srcap.unsqueeze(1).broadcast_to([16, 8, E])
                kb.dma("sync", scf[:, 2048 * cc:2048 * (cc + 1)], srcap, scsem)
        return (ldsem, kb.cnt[ldsem], scsem, kb.cnt[scsem])

    load_tgt = {0: emit_phase_loads(*phases[0], slot=0)}

    SH_S = (12, 8, 4, 0)
    SH_M = (9, 5, 1, -3)

    SH_S2 = (12, 8, 4, 0)

    def make_jobs(pi):
        kind, w = phases[pi]
        slot = pi % 2
        nch, width = (4, FW) if kind in ("g", "u") else (2, 2048)
        tiles = [(c, i) for c in range(nch) for i in range(4)]
        jobs = []
        for idx, (c, i) in enumerate(tiles):
            def job(idx=idx, c=c, i=i, slot=slot, width=width):
                sh_s, sh_m = SH_S[i], SH_M[i]
                op = (ALU.logical_shift_left if sh_m >= 0
                      else ALU.logical_shift_right)
                for off in range(0, width, 512):
                    ws = min(512, width - off)
                    tl = (4 * c + i) * width + off
                    pk_ap = pkbuf[:, 4096 * slot + width * c + off:
                                  4096 * slot + width * c + off + ws]
                    s16_ap = s16buf[:, 4096 * slot + width * c + off:
                                    4096 * slot + width * c + off + ws]
                    d1 = dq1[:, 0:ws]
                    d2 = dq2[:, 0:ws]
                    kb.step("vector", lambda v, pk=pk_ap, sh=sh_s, d1=d1:
                            v.tensor_scalar(out=d1, in0=pk, scalar1=sh,
                                            scalar2=0x8000,
                                            op0=ALU.logical_shift_left,
                                            op1=ALU.bitwise_and))
                    kb.step("vector", lambda v, pk=pk_ap, sh=abs(sh_m), op=op,
                            d2=d2: v.tensor_scalar(out=d2, in0=pk, scalar1=sh,
                                                   scalar2=0x0E00, op0=op,
                                                   op1=ALU.bitwise_and))
                    kb.step("vector", lambda v, tl=tl, d1=d1, d2=d2, slot=slot,
                            ws=ws: v.tensor_tensor(
                                out=w4buf[slot][:, tl:tl + ws], in0=d1, in1=d2,
                                op=ALU.bitwise_or))
                    kb.step("vector", lambda v, tl=tl, s16=s16_ap, slot=slot,
                            ws=ws: v.tensor_tensor(
                                out=w4buf[slot][:, tl:tl + ws].bitcast(BF16),
                                in0=w4buf[slot][:, tl:tl + ws].bitcast(FP16),
                                in1=s16, op=ALU.mult), sem="sV")
            jobs.append(job)
        return jobs

    from collections import deque
    mm_tgt = {}
    dq_done = {}
    queued = {}
    dq_fin = set()

    mult_tgt = {}

    def queue_phase(p):
        if p in queued or p in dq_fin or p >= len(phases):
            return
        ldsem, ldc, scsem, scc = load_tgt[p]
        slot = p % 2
        if p >= 2:
            kb.wait("scalar", "sV", dq_done[p - 2])
        kb.wait("scalar", scsem, scc)
        mult_tgt[p] = kb.step("scalar", lambda s, slot=slot: s.activation(
            out=s16buf[:, 4096 * slot:4096 * (slot + 1)], in_=scf[:, 0:4096],
            func=AF.Copy, scale=16384.0), sem="sS")
        kb.wait("vector", ldsem, ldc)
        kb.wait("vector", "sS", mult_tgt[p])
        if p >= 2:
            kb.wait("vector", "sT", mm_tgt[p - 2])
        queued[p] = deque(make_jobs(p))

    queue_phase(0)
    kb.wait("sync", "sS", mult_tgt[0])
    load_tgt[1] = emit_phase_loads(*phases[1], slot=1)

    def pop_jobs(p, n):
        q = queued.get(p)
        while q and n > 0:
            q.popleft()()
            n -= 1
        if q is not None and not q:
            dq_done[p] = kb.cnt["sV"]
            dq_fin.add(p)
            del queued[p]

    prev_gu = None
    prev_d = None
    for pi, (kind, w) in enumerate(phases):
        slot = pi % 2
        queue_phase(pi)
        pop_jobs(pi, 99)
        kb.wait("tensor", "sV", dq_done[pi])
        queue_phase(pi + 1)
        if pi + 2 < len(phases):
            kb.wait("sync", "sV", dq_done[pi])
            kb.wait("sync", "sS", mult_tgt[pi + 1])
            load_tgt[pi + 2] = emit_phase_loads(*phases[pi + 2], slot=(pi + 2) % 2)
        if kind in ("g", "u"):
            if kind == "u":
                kb.wait("vector", "sS")
            evsem = "sS" if kind == "g" else "sV"
            evac_tgt = {}
            gi = 0
            for nt in range(4):
                for tb in range(2):
                    j = gi % 4
                    if gi >= 4:
                        kb.wait("tensor", evsem, evac_tgt[gi - 4])
                    elif prev_gu is not None:
                        pv_evac, pv_sem, pv_n = prev_gu
                        kb.wait("tensor", pv_sem, pv_evac[pv_n - 4 + gi])
                    for ci in range(16):
                        w_ap = w4buf[slot][:, ci * FW + 128 * nt:
                                           ci * FW + 128 * (nt + 1)].bitcast(BF16)
                        b_ap = b_ffn[:, TPC * ci + 512 * tb:
                                     TPC * ci + 512 * (tb + 1)]
                        kb.step("tensor", lambda t, wv=w_ap, b=b_ap, ci=ci, j=j:
                                t.matmul(out=ps_gu[j][:], lhsT=wv, rhs=b,
                                         start=(ci == 0), stop=(ci == 15)),
                                sem="sT" if ci == 15 else None)
                    gst = 4 * (w % 2) + nt
                    gsl = gwin[:, TPC * gst + 512 * tb: TPC * gst + 512 * (tb + 1)]
                    if kind == "g":
                        kb.wait("scalar", "sT")
                        evac_tgt[gi] = kb.step(
                            "scalar", lambda s, gsl=gsl, j=j: s.activation(
                                out=gsl, in_=ps_gu[j][:],
                                func=AF.Gelu_apprx_tanh), sem="sS")
                    else:
                        kb.wait("vector", "sT")
                        evac_tgt[gi] = kb.step(
                            "vector", lambda v, gsl=gsl, j=j: v.tensor_tensor(
                                out=gsl, in0=gsl, in1=ps_gu[j][:], op=ALU.mult),
                            sem="sV")
                    gi += 1
                    pop_jobs(pi + 1, 3)
            prev_gu = (evac_tgt, evsem, 8)
        else:
            evac_tgt = {}
            gi = 0
            for hc in range(16):
                for tb in range(2):
                    j = gi % 4
                    if gi >= 4:
                        kb.wait("tensor", "sV", evac_tgt[gi - 4])
                    elif prev_d is not None:
                        pv_evac, pv_sem, pv_n = prev_d
                        kb.wait("tensor", pv_sem, pv_evac[pv_n - 4 + gi])
                    for sub in range(8):
                        w_ap = w4buf[slot][:, sub * E + 128 * hc:
                                           sub * E + 128 * (hc + 1)].bitcast(BF16)
                        h_ap = gwin[:, TPC * sub + 512 * tb:
                                    TPC * sub + 512 * (tb + 1)]
                        kb.step("tensor", lambda t, wv=w_ap, h=h_ap, sub=sub, j=j:
                                t.matmul(out=ps_dn[j][:], lhsT=wv, rhs=h,
                                         start=(sub == 0), stop=(sub == 7)),
                                sem="sT" if sub == 7 else None)
                    kb.wait("vector", "sT")
                    fpar = gi % 2
                    stsem = "qG" if fpar == 0 else "qH"
                    kb.wait("vector", stsem)
                    evac_tgt[gi] = kb.step(
                        "vector", lambda v, fpar=fpar, j=j: v.tensor_copy(
                            out=fstage[:, 512 * fpar:512 * (fpar + 1)],
                            in_=ps_dn[j][:]), sem="sV")
                    kb.wait("gpsimd", "sV", evac_tgt[gi])
                    dst = acc_dram[128 * hc:128 * (hc + 1),
                                   512 * tb:512 * (tb + 1)]
                    if w == 0:
                        kb.step("gpsimd", lambda g2, dst=dst, fpar=fpar:
                                g2.dma_start(out=dst,
                                             in_=fstage[:, 512 * fpar:512 * (fpar + 1)]),
                                sem=stsem, k=16)
                    else:
                        kb.step("gpsimd", lambda g2, dst=dst, fpar=fpar:
                                g2.dma_start(out=dst,
                                             in_=fstage[:, 512 * fpar:512 * (fpar + 1)],
                                             accum_op=ALU.add), sem=stsem, k=16)
                    if w == FWINS // 2 - 1 and tb == 0:
                        ld5 = "qL" if (hc % 2) == 0 else "qM"
                        kb.wait("sync", stsem)
                        kb.dma("sync", acc_sb[:, 512 * hc:512 * (hc + 1)],
                               acc_dram[128 * hc:128 * (hc + 1), 0:512], ld5)
                    gi += 1
                    pop_jobs(pi + 1, 2 if gi <= 16 else 1)
            prev_d = (evac_tgt, "sV", 32)
        mm_tgt[pi] = kb.cnt["sT"]

    sync_fence()
    es4a.close()

    if DEBUG:
        acc_dbg = nc.dram_tensor("acc_dbg", [E, TPC], F32,
                                 kind="ExternalOutput").ap()
        kb.wait("gpsimd", "qG")
        kb.wait("gpsimd", "qH")
        kb.dma("gpsimd", acc_dbg, acc_dram, "qG")
        kb.wait("gpsimd", "qG")
        kb.barrier()

    x1r = es4.enter_context(nc.sbuf_tensor("x1r", [128, 16 * 512], BF16))
    aBs = es4.enter_context(nc.sbuf_tensor("aBs", [128, 16 * 512], BF16))
    sq5 = es4.enter_context(nc.sbuf_tensor("sq5", [128, 16 * 512], BF16))
    mB5 = es4.enter_context(nc.sbuf_tensor("mB5", [128, 512], F32))
    rB5 = es4.enter_context(nc.sbuf_tensor("rB5", [128, 512], F32))
    ot1 = es4.enter_context(nc.sbuf_tensor("ot1", [128, 2 * 512], F32))
    ps_s5 = es4.enter_context(nc.psum_tensor("ps_s5", [1, 512], F32))
    ps_s25 = es4.enter_context(nc.psum_tensor("ps_s25", [1, 512], F32))
    ps_bc5 = es4.enter_context(nc.psum_tensor("ps_bc5", [128, 512], F32))

    for tb in range(2):
        if tb == 0:
            kb.wait("sync", "qG")
            kb.wait("sync", "qH")
        else:
            kb.wait("sync", "sV")
        x1sem = "qD" if tb == 0 else "qE"
        if tb == 1:
            for ec in range(16):
                ldsem5 = "qL" if (ec % 2) == 0 else "qM"
                kb.dma("sync", acc_sb[:, 512 * ec:512 * (ec + 1)],
                       acc_dram[128 * ec:128 * (ec + 1), 512:1024], ldsem5)
        for ec in range(16):
            kb.dma("sync", x1r[:, 512 * ec:512 * (ec + 1)],
                   x1b_sp[128 * ec:128 * (ec + 1), 512 * tb:512 * (tb + 1)],
                   x1sem)
        kb.wait("vector", "qL")
        kb.wait("vector", "qM")
        kb.wait("vector", "sT")
        cp_tgt = {}
        for ec in range(16):
            kb.step("vector", lambda v, ec=ec: v.tensor_copy(
                out=aBs[:, 512 * ec:512 * (ec + 1)],
                in_=acc_sb[:, 512 * ec:512 * (ec + 1)]), sem="sV")
            cp_tgt[ec] = kb.cnt["sV"]
        kb.wait("scalar", "sT")
        sq_tgt = {}
        for ec in range(16):
            kb.wait("scalar", "sV", cp_tgt[ec])
            kb.step("scalar", lambda s, ec=ec: s.activation(
                out=sq5[:, 512 * ec:512 * (ec + 1)],
                in_=aBs[:, 512 * ec:512 * (ec + 1)],
                func=AF.Square), sem="sS")
            sq_tgt[ec] = kb.cnt["sS"]
        for ec in range(16):
            kb.wait("tensor", "sV", cp_tgt[ec])
            kb.step("tensor", lambda t, ec=ec: t.matmul(
                out=ps_s5[0:1, :], lhsT=ones128[:],
                rhs=aBs[:, 512 * ec:512 * (ec + 1)],
                start=(ec == 0), stop=(ec == 15)))
            kb.wait("tensor", "sS", sq_tgt[ec])
            kb.step("tensor", lambda t, ec=ec: t.matmul(
                out=ps_s25[0:1, :], lhsT=ones128[:],
                rhs=sq5[:, 512 * ec:512 * (ec + 1)],
                start=(ec == 0), stop=(ec == 15)), sem="sT")
        kb.wait("vector", "sT")
        kb.step("vector", lambda v: v.tensor_scalar_mul(
            rw_m[0:1, 0:512], ps_s5[0:1, :], R_E))
        kb.step("vector", lambda v: v.tensor_tensor(
            out=rw_t[0:1, 0:512], in0=rw_m[0:1, 0:512], in1=rw_m[0:1, 0:512],
            op=ALU.mult))
        kb.step("vector", lambda v: v.scalar_tensor_tensor(
            out=rw_v[0:1, 0:512], in0=ps_s25[0:1, :], scalar=R_E,
            in1=rw_t[0:1, 0:512], op0=ALU.mult, op1=ALU.subtract), sem="sV")
        kb.wait("scalar", "sV")
        kb.step("scalar", lambda s: s.activation(
            out=rw_v[0:1, 0:512], in_=rw_v[0:1, 0:512], func=AF.Sqrt, bias=epsb[0:1, 0:1]),
            sem="sS")
        kb.wait("vector", "sS")
        kb.step("vector", lambda v: v.reciprocal(
            out=rw_v[0:1, 0:512], in_=rw_v[0:1, 0:512]), sem="sV")
        kb.wait("tensor", "sV")
        kb.step("tensor", lambda t: t.matmul(
            out=ps_bc5[:], lhsT=ones1f[:], rhs=rw_m[0:1, 0:512],
            start=True, stop=True), sem="sT")
        kb.wait("vector", "sT")
        kb.step("vector", lambda v: v.tensor_copy(out=mB5[:], in_=ps_bc5[:]),
                sem="sV")
        kb.wait("tensor", "sV")
        kb.step("tensor", lambda t: t.matmul(
            out=ps_bc5[:], lhsT=ones1f[:], rhs=rw_v[0:1, 0:512],
            start=True, stop=True), sem="sT")
        kb.wait("vector", "sT")
        kb.step("vector", lambda v: v.tensor_copy(out=rB5[:], in_=ps_bc5[:]),
                sem="sV")
        kb.wait("vector", x1sem)
        kb.wait("vector", "qG")
        kb.wait("vector", "qH")
        for ec in range(16):
            slot = 512 * (ec % 2)
            asl = acc_sb[:, 512 * ec:512 * (ec + 1)]
            if ec >= 2:
                stsem_p = "qG" if (ec % 2) == 0 else "qH"
                kb.wait("vector", stsem_p)
            kb.step("vector", lambda v, asl=asl, slot=slot: v.tensor_tensor(
                out=ot1[:, slot:slot + 512], in0=asl, in1=mB5[:],
                op=ALU.subtract))
            kb.step("vector", lambda v, slot=slot: v.tensor_tensor(
                out=ot1[:, slot:slot + 512], in0=ot1[:, slot:slot + 512],
                in1=rB5[:], op=ALU.mult))
            kb.step("vector", lambda v, slot=slot, ec=ec: v.tensor_tensor(
                out=ot1[:, slot:slot + 512], in0=ot1[:, slot:slot + 512],
                in1=x1r[:, 512 * ec:512 * (ec + 1)], op=ALU.add), sem="sV")
            stsem = "qG" if (ec % 2) == 0 else "qH"
            kb.wait("gpsimd", "sV")
            kb.dma("gpsimd",
                   out_d[128 * ec:128 * (ec + 1), 512 * tb:512 * (tb + 1)],
                   ot1[:, slot:slot + 512], stsem)

    kb.wait("gpsimd", "qG")
    kb.wait("gpsimd", "qH")

    with nc.Block() as block:
        block.sync(lambda e: [f(e) for f in kb.prog["sync"]])
        block.vector(lambda e: [f(e) for f in kb.prog["vector"]])
        block.scalar(lambda e: [f(e) for f in kb.prog["scalar"]])
        block.tensor(lambda e: [f(e) for f in kb.prog["tensor"]])
        block.gpsimd(lambda e: [f(e) for f in kb.prog["gpsimd"]])
    es4.close()
    es34.close()
    es.close()
    return nc



def kernel(**inputs):
    d = _hostprep(inputs)
    nc = build_program()
    in_maps = []
    for r in range(NC):
        in_maps.append(dict(
            x0T_own=d["x0T_own"][r].astype(np.float32),
            wqkvT=d["wqkvT"][r].astype(np.float32),
            wpT_own=d["wpT_own"][r].astype(np.float32),
            pk_g=d["pk_g"], pk_u=d["pk_u"], pk_d=d["pk_d"],
            scT_g=d["scT_g"].astype(np.float32),
            scT_u=d["scT_u"].astype(np.float32),
            scT_d=d["scT_d"].astype(np.float32),
            CS1=d["CS1"], CS2=d["CS2"],
            maskM=d["maskM"], ident=d["ident"],
        ))
    res = run_bass_kernel_spmd(nc, in_maps, core_ids=list(range(NC)))
    big = np.empty((E, T), dtype=np.float32)
    for r in range(NC):
        big[:, d["own_cols"][r]] = res.results[r]["out"]
    out = np.empty((T, E), dtype=np.float32)
    out[:, PI_E] = big.T
    return out[None]


# version 15
# speedup vs baseline: 1.0153x; 1.0153x over previous
import numpy as np
import ml_dtypes
from contextlib import ExitStack

import concourse.bass as bass
import concourse.mybir as mybir
from concourse.bass_utils import run_bass_kernel_spmd

F32 = mybir.dt.float32
BF16 = mybir.dt.bfloat16
FP16 = mybir.dt.float16
U16 = mybir.dt.uint16
AF = mybir.ActivationFunctionType
ALU = mybir.AluOpType
NPBF16 = ml_dtypes.bfloat16

E = 2048
T = 8192
H = 16
D = 128
MLP = 16384
NC = 8
EPS = 1e-5
TPC = T // NC
NG = 4
GT = T // NG
PC = GT // NC
NWIN = 1024
NWINS = MLP // NWIN
C_SCALE = float(np.float32(1.0) / np.sqrt(np.float32(128.0)))
R_E = float(np.float32(1.0 / E))

ENGS = ("sync", "vector", "scalar", "tensor", "gpsimd")
DEBUG = False



def _perm_enum(total):
    out = np.empty(total, dtype=np.int64)
    for c in range(total // 512):
        for i in range(4):
            for p in range(128):
                out[512 * c + 128 * i + p] = 512 * c + 4 * p + i
    return out


PI_E = _perm_enum(E)
PI_N = _perm_enum(MLP)
PI_D = np.concatenate([np.arange(0, 128, 2), np.arange(1, 128, 2)])


def _pack_tiles(mat, rt, ct):
    R, C = mat.shape
    return np.ascontiguousarray(
        mat.reshape(R // rt, rt, C // ct, ct).transpose(0, 2, 1, 3))


def _prep_packed(W_packed_int32, row_perm):
    P8 = np.asarray(W_packed_int32).astype(np.uint8)[row_perm]
    U = P8.view(np.uint16)
    Tt = np.ascontiguousarray(U.T)
    K4 = Tt.shape[0]
    return np.ascontiguousarray(Tt.reshape(K4 // 128, 128, -1))


def _hostprep(inp):
    d = {}
    x = np.asarray(inp["L0_attn_x"])[0]
    x0T = np.ascontiguousarray(x.T)[PI_E]
    own_cols = [np.concatenate([np.arange(GT * g + PC * r, GT * g + PC * (r + 1))
                                for g in range(NG)]) for r in range(NC)]
    d["own_cols"] = own_cols
    d["x0T_own"] = [np.ascontiguousarray(x0T[:, own_cols[r]]) for r in range(NC)]

    Wqkv = np.asarray(inp["L0_attn_W_qkv"])
    Wq = Wqkv[0:E].reshape(H, D, E)[:, PI_D].reshape(E, E)
    Wk = Wqkv[E:2 * E].reshape(H, D, E)[:, PI_D].reshape(E, E)
    Wv = Wqkv[2 * E:3 * E]
    d["wqkvT"] = []
    for r in range(NC):
        Wf = np.concatenate([W[D * 2 * r: D * (2 * r + 2)] for W in (Wq, Wk, Wv)], 0)
        WfT = np.ascontiguousarray(Wf[:, PI_E].T)
        tl = _pack_tiles(WfT, 128, 128)
        d["wqkvT"].append(np.ascontiguousarray(
            tl.transpose(0, 2, 1, 3).reshape(16, 128, 6 * 128)))

    Wp = np.asarray(inp["L0_attn_W_proj"])[PI_E]
    wpT = _pack_tiles(np.ascontiguousarray(Wp.T), 128, 128)
    d["wpT_own"] = [np.ascontiguousarray(
        wpT[2 * r:2 * r + 2].transpose(0, 2, 1, 3).reshape(2, 128, 16 * 128))
        for r in range(NC)]

    d["pk_g"] = _prep_packed(inp["L0_ffn_gate_W_packed"], PI_N)
    d["pk_u"] = _prep_packed(inp["L0_ffn_up_W_packed"], PI_N)
    d["pk_d"] = _prep_packed(inp["L0_ffn_down_W_packed"], PI_E)
    d["scT_g"] = np.ascontiguousarray(np.asarray(inp["L0_ffn_gate_scales"])[PI_N].T)
    d["scT_u"] = np.ascontiguousarray(np.asarray(inp["L0_ffn_up_scales"])[PI_N].T)
    d["scT_d"] = np.ascontiguousarray(np.asarray(inp["L0_ffn_down_scales"])[PI_E].T)

    inv = 1.0 / (10000.0 ** (np.arange(0, D, 2, dtype=np.float64) / D))
    ang = np.outer(inv, np.arange(T, dtype=np.float64))
    cos = np.cos(ang).astype(np.float32)
    sin = np.sin(ang).astype(np.float32)
    d["CS1"] = np.ascontiguousarray(np.concatenate([cos, cos], 0))
    d["CS2"] = np.ascontiguousarray(np.concatenate([-sin, sin], 0))

    cgrid = np.arange(896)[None, :]
    igrid = np.arange(128)[:, None]
    d["maskM"] = (cgrid >= igrid + 384).astype(NPBF16)
    d["ident"] = np.eye(128, dtype=NPBF16)
    return d



class KB:
    def __init__(self, nc, es):
        self.nc = nc
        self.es = es
        self.prog = {e: [] for e in ENGS}
        self.sems = {}
        self.cnt = {}

    def sem(self, *names):
        for n in names:
            self.sems[n] = self.es.enter_context(self.nc.semaphore(n))
            self.cnt[n] = 0

    def step(self, eng, fn, sem=None, k=1):
        if sem is not None:
            self.cnt[sem] += k
            tgt = self.cnt[sem]
            s = self.sems[sem]

            def go(e, fn=fn, s=s, k=k):
                fn(e).then_inc(s, k)
            self.prog[eng].append(go)
            return tgt
        self.prog[eng].append(lambda e, fn=fn: fn(e))
        return None

    def dma(self, eng, out, in_, sem):
        return self.step(eng, lambda e, o=out, i=in_: e.dma_start(out=o, in_=i),
                         sem=sem, k=16)

    def wait(self, eng, sem, val=None):
        v = self.cnt[sem] if val is None else val
        if v <= 0:
            return
        s = self.sems[sem]
        self.prog[eng].append(lambda e, s=s, v=v: e.wait_ge(s, v))

    def barrier(self):
        for e in ENGS:
            self.step(e, lambda eng: eng.nop(), sem="b_" + e, k=1)
        snap = {e: self.cnt["b_" + e] for e in ENGS}
        for e in ENGS:
            for o in ENGS:
                if o != e:
                    self.wait(e, "b_" + o, snap[o])



def build_program():
    nc = bass.Bass("TRN2", target_bir_lowering=False)
    es = ExitStack()
    kb = KB(nc, es)
    kb.sem("qL", "qM", "qG", "qH", "qC", "qD", "qE", "qF", "sV", "sS", "sT", "sG", "sCC", *["b_" + e for e in ENGS])

    def sync_fence():
        kb.wait("sync", "sT")
        kb.wait("sync", "sV")
        kb.wait("sync", "qG")
        kb.wait("sync", "qH")

    din = lambda name, shape, dtype: nc.dram_tensor(
        name, shape, dtype, kind="ExternalInput").ap()
    x0_d = din("x0T_own", [E, TPC], F32)
    wqkv_d = din("wqkvT", [16, 128, 6 * 128], F32)
    wp_d = din("wpT_own", [2, 128, 16 * 128], F32)
    pkg_d = din("pk_g", [4, 128, MLP], U16)
    pku_d = din("pk_u", [4, 128, MLP], U16)
    pkd_d = din("pk_d", [32, 128, E], U16)
    scg_d = din("scT_g", [64, MLP], F32)
    scu_d = din("scT_u", [64, MLP], F32)
    scd_d = din("scT_d", [512, E], F32)
    cs1_d = din("CS1", [128, T], F32)
    cs2_d = din("CS2", [128, T], F32)
    mask_d = din("maskM", [128, 896], BF16)
    ident_d = din("ident", [128, 128], BF16)

    out_d = nc.dram_tensor("out", [E, TPC], F32, kind="ExternalOutput").ap()

    ikind = "ExternalOutput" if DEBUG else "Internal"
    cc_b0_in = [nc.dram_tensor(f"cc_b0_in{h}", [E, 512], BF16,
                               kind="Internal").ap() for h in range(2)]
    cc_b0_out = [nc.dram_tensor(f"cc_b0_out{h}", [NC * E, 512], BF16,
                                addr_space="Shared", kind="Internal").ap()
                 for h in range(2)]
    rs_in = [nc.dram_tensor(f"rs_in{g}", [NC * E, PC], F32, kind="Internal").ap()
             for g in range(NG)]
    rs_out = [nc.dram_tensor(f"rs_out{g}", [E, PC], F32, kind="Internal").ap()
              for g in range(NG)]
    x0b_sp = nc.dram_tensor("x0b_sp", [E, TPC], BF16, kind=ikind).ap()
    acc_dram = nc.dram_tensor("acc_dram", [E, TPC], F32, kind="Internal").ap()
    x1b_sp = nc.dram_tensor("x1b_sp", [E, TPC], BF16, kind=ikind).ap()

    ones128 = es.enter_context(nc.sbuf_tensor("ones128", [128, 1], BF16))
    ones1 = es.enter_context(nc.sbuf_tensor("ones1", [1, 128], BF16))
    ones1f = es.enter_context(nc.sbuf_tensor("ones1f", [1, 128], F32))
    rw_m = es.enter_context(nc.sbuf_tensor("rw_m", [1, 512], F32))
    rw_v = es.enter_context(nc.sbuf_tensor("rw_v", [1, 512], F32))
    rw_t = es.enter_context(nc.sbuf_tensor("rw_t", [1, 512], F32))
    rw_mb = es.enter_context(nc.sbuf_tensor("rw_mb", [1, 512], BF16))
    rw_rb = es.enter_context(nc.sbuf_tensor("rw_rb", [1, 512], BF16))
    epsb = es.enter_context(nc.sbuf_tensor("epsb", [1, 1], F32))

    kb.step("vector", lambda v: v.memset(ones128[:], 1.0))
    kb.step("vector", lambda v: v.memset(ones1[:], 1.0))
    kb.step("vector", lambda v: v.memset(ones1f[:], 1.0))
    kb.step("vector", lambda v: v.memset(epsb[:], EPS), sem="sV")

    def ln_stats(strips, tw, ps_s, ps_s2, ps_bc, sqst, mB, rB, f32out=False):
        nstr = len(strips)
        kb.wait("tensor", "sV")
        kb.wait("scalar", "sV")
        s2_tgt = {}
        for j, xs in enumerate(strips):
            kb.step("tensor", lambda t, xs=xs, j=j: t.matmul(
                out=ps_s[0:1, 0:tw], lhsT=ones128[:], rhs=xs,
                start=(j == 0), stop=(j == nstr - 1)), sem="sT")
            slot = (j % 4) * tw
            if j >= 4:
                kb.wait("scalar", "sT", s2_tgt[j - 4])
            kb.step("scalar", lambda s, xs=xs, slot=slot: s.activation(
                out=sqst[:, slot:slot + tw], in_=xs, func=AF.Square), sem="sS")
            kb.wait("tensor", "sS")
            s2_tgt[j] = kb.step("tensor", lambda t, slot=slot, j=j: t.matmul(
                out=ps_s2[0:1, 0:tw], lhsT=ones128[:], rhs=sqst[:, slot:slot + tw],
                start=(j == 0), stop=(j == nstr - 1)), sem="sT")
        kb.wait("vector", "sT")
        kb.step("vector", lambda v: v.tensor_scalar_mul(
            rw_m[0:1, 0:tw], ps_s[0:1, 0:tw], R_E))
        kb.step("vector", lambda v: v.tensor_tensor(
            out=rw_t[0:1, 0:tw], in0=rw_m[0:1, 0:tw], in1=rw_m[0:1, 0:tw],
            op=ALU.mult))
        kb.step("vector", lambda v: v.scalar_tensor_tensor(
            out=rw_v[0:1, 0:tw], in0=ps_s2[0:1, 0:tw], scalar=R_E,
            in1=rw_t[0:1, 0:tw], op0=ALU.mult, op1=ALU.subtract), sem="sV")
        kb.wait("scalar", "sV")
        kb.step("scalar", lambda s: s.activation(
            out=rw_v[0:1, 0:tw], in_=rw_v[0:1, 0:tw], func=AF.Sqrt, bias=epsb[0:1, 0:1]),
            sem="sS")
        kb.wait("vector", "sS")
        kb.step("vector", lambda v: v.reciprocal(
            out=rw_v[0:1, 0:tw], in_=rw_v[0:1, 0:tw]))
        if f32out:
            kb.step("vector", lambda v: v.engine_nop(), sem="sV")
            m_rhs, r_rhs, l_ones = rw_m, rw_v, ones1f
        else:
            kb.step("vector", lambda v: v.tensor_copy(
                out=rw_mb[0:1, 0:tw], in_=rw_m[0:1, 0:tw]))
            kb.step("vector", lambda v: v.tensor_copy(
                out=rw_rb[0:1, 0:tw], in_=rw_v[0:1, 0:tw]), sem="sV")
            m_rhs, r_rhs, l_ones = rw_mb, rw_rb, ones1
        kb.wait("tensor", "sV")
        kb.step("tensor", lambda t: t.matmul(
            out=ps_bc[:, 0:tw], lhsT=l_ones[:], rhs=m_rhs[0:1, 0:tw],
            start=True, stop=True), sem="sT")
        kb.wait("vector", "sT")
        kb.step("vector", lambda v: v.tensor_copy(
            out=mB[:, 0:tw], in_=ps_bc[:, 0:tw]), sem="sV")
        kb.wait("tensor", "sV")
        kb.step("tensor", lambda t: t.matmul(
            out=ps_bc[:, 0:tw], lhsT=l_ones[:], rhs=r_rhs[0:1, 0:tw],
            start=True, stop=True), sem="sT")
        kb.wait("vector", "sT")
        kb.step("vector", lambda v: v.tensor_copy(
            out=rB[:, 0:tw], in_=ps_bc[:, 0:tw]), sem="sV")

    es_w = ExitStack()
    wqkvb = es_w.enter_context(nc.sbuf_tensor("wqkvb", [128, 6 * 16 * 128], BF16))
    wpb = es_w.enter_context(nc.sbuf_tensor("wpb", [128, 2 * 16 * 128], BF16))
    maskM = es_w.enter_context(nc.sbuf_tensor("maskM_s", [128, 896], BF16))
    identb = es_w.enter_context(nc.sbuf_tensor("identb", [128, 128], BF16))

    kb.dma("sync", identb[:], ident_d, "qC")
    kb.dma("sync", maskM[:], mask_d, "qC")
    kb.wait("tensor", "qC")
    kb.wait("vector", "qC")

    es0 = ExitStack()
    p0f = es0.enter_context(nc.sbuf_tensor("p0f", [128, 2 * 2048], F32))
    x0bs = es0.enter_context(nc.sbuf_tensor("x0bs", [128, 16 * TPC], BF16))
    sqst0 = es0.enter_context(nc.sbuf_tensor("sqst0", [128, 2048], BF16))
    nrm0 = es0.enter_context(nc.sbuf_tensor("nrm0", [128, 2 * 512], BF16))
    mB0 = es0.enter_context(nc.sbuf_tensor("mB0", [128, 512], BF16))
    rB0 = es0.enter_context(nc.sbuf_tensor("rB0", [128, 512], BF16))
    ps_s = es0.enter_context(nc.psum_tensor("ps_s", [1, 512], F32))
    ps_s2 = es0.enter_context(nc.psum_tensor("ps_s2", [1, 512], F32))
    ps_bc = es0.enter_context(nc.psum_tensor("ps_bc", [128, 512], F32))

    tile_jobs = []
    for ec in range(16):
        dst = wqkvb[:].rearrange("p (f e c) -> p f e c", f=6, e=16)[:, :, ec, :]
        tile_jobs.append((wqkv_d[ec], dst, 768))
    for e2 in range(2):
        dst = wpb[:, e2 * 2048:(e2 + 1) * 2048].rearrange(
            "p (f c) -> p f c", f=16)
        tile_jobs.append((wp_d[e2], dst, 2048))
    for j, (srcap, dst, w) in enumerate(tile_jobs):
        par = j % 2
        ldsem = "qL" if par == 0 else "qM"
        if j >= 2:
            kb.wait("sync", "sV")
        kb.dma("sync", p0f[:, 2048 * par:2048 * par + w], srcap, ldsem)
        kb.wait("vector", ldsem)
        kb.step("vector", lambda v, par=par, dst=dst, w=w: v.tensor_copy(
            out=dst, in_=p0f[:, 2048 * par:2048 * par + w].rearrange(
                "p (a c) -> p a c", c=128)), sem="sV")


    for ec in range(16):
        par = ec % 2
        ldsem = "qL" if par == 0 else "qM"
        kb.wait("sync", "sV")
        kb.dma("sync", p0f[:, 2048 * par:2048 * par + 1024],
               x0_d[128 * ec:128 * (ec + 1), :], ldsem)
        kb.wait("vector", ldsem)
        kb.step("vector", lambda v, ec=ec, par=par: v.tensor_copy(
            out=x0bs[:, TPC * ec:TPC * (ec + 1)],
            in_=p0f[:, 2048 * par:2048 * par + 1024]), sem="sV")
    kb.wait("gpsimd", "sV")
    for ec in range(16):
        kb.dma("gpsimd", x0b_sp[128 * ec:128 * (ec + 1), :],
               x0bs[:, TPC * ec:TPC * (ec + 1)], "qG")

    ag_tgt = []
    for tb in range(2):
        strips = [x0bs[:, TPC * ec + 512 * tb: TPC * ec + 512 * (tb + 1)]
                  for ec in range(16)]
        ln_stats(strips, 512, ps_s, ps_s2, ps_bc, sqst0, mB0, rB0)
        for ec in range(16):
            par = ec % 2
            src = strips[ec]
            stsem = "qG" if par == 0 else "qH"
            kb.wait("vector", stsem)
            kb.step("vector", lambda v, src=src, par=par: v.tensor_tensor(
                out=nrm0[:, 512 * par:512 * (par + 1)], in0=src, in1=mB0[:],
                op=ALU.subtract))
            kb.step("vector", lambda v, par=par: v.tensor_tensor(
                out=nrm0[:, 512 * par:512 * (par + 1)],
                in0=nrm0[:, 512 * par:512 * (par + 1)], in1=rB0[:],
                op=ALU.mult), sem="sV")
            kb.wait("gpsimd", "sV")
            kb.dma("gpsimd",
                   cc_b0_in[tb][128 * ec:128 * (ec + 1), :],
                   nrm0[:, 512 * par:512 * (par + 1)], stsem)
        kb.wait("gpsimd", "qG")
        kb.wait("gpsimd", "qH")
        kb.step("gpsimd", lambda g, tb=tb: g.collective_compute(
            "AllGather", ALU.bypass, ins=[cc_b0_in[tb]], outs=[cc_b0_out[tb]],
            replica_groups=[list(range(NC))]), sem="sCC")
        ag_tgt.append(kb.cnt["sCC"])

    sync_fence()
    es0.close()

    es12 = ExitStack()
    Kt = es12.enter_context(nc.sbuf_tensor("Kt", [128, 2 * T], BF16))
    Vt = es12.enter_context(nc.sbuf_tensor("Vt", [128, 2 * T], BF16))
    Qt = es12.enter_context(nc.sbuf_tensor("Qt", [128, 2 * T], BF16))

    es1 = ExitStack()
    b0c = es1.enter_context(nc.sbuf_tensor("b0c", [128, 2 * 16 * 512], BF16))
    zb = es1.enter_context(nc.sbuf_tensor("zb", [128, 6 * 512], BF16))
    rt1 = es1.enter_context(nc.sbuf_tensor("rt1", [128, 512], BF16))
    rt2 = es1.enter_context(nc.sbuf_tensor("rt2", [128, 512], BF16))
    csb = es1.enter_context(nc.sbuf_tensor("csb", [128, 2 * 1024], BF16))
    csf = es1.enter_context(nc.sbuf_tensor("csf", [128, 2 * 1024], F32))
    ps_qkv = [es1.enter_context(nc.psum_tensor(f"ps_qkv{f}", [128, 512], F32))
              for f in range(6)]
    ps_vt = es1.enter_context(nc.psum_tensor("ps_vt", [128, 512], BF16))


    evac_done = 0
    evac_f = {}
    b0_loaded = {}
    mm_done = {}
    cs_cast = {}
    for tc in range(16 + 1):
        par = tc % 2
        if tc < 16:
            ldsem = "qL" if par == 0 else "qM"
            if tc >= 2:
                kb.wait("sync", "sT", mm_done[tc - 2])
            r0 = 2 * (tc % 4)
            half = tc // 8
            lc = 256 * (tc // 4) - 512 * half
            kb.wait("sync", "sCC", ag_tgt[half])
            for ec in range(16):
                for rr in range(2):
                    kb.dma("sync",
                           b0c[:, (par * 16 + ec) * 512 + 256 * rr:
                               (par * 16 + ec) * 512 + 256 * (rr + 1)],
                           cc_b0_out[half][E * (r0 + rr) + 128 * ec:
                                           E * (r0 + rr) + 128 * (ec + 1),
                                           lc:lc + 256],
                           ldsem)
            if tc >= 2:
                kb.wait("sync", "sV", cs_cast.get(tc - 2, 0))
            kb.dma("sync", csf[:, 1024 * par:1024 * par + 512],
                   cs1_d[:, 512 * tc:512 * (tc + 1)], ldsem)
            kb.dma("sync", csf[:, 1024 * par + 512:1024 * (par + 1)],
                   cs2_d[:, 512 * tc:512 * (tc + 1)], ldsem)
            b0_loaded[tc] = (ldsem, kb.cnt[ldsem])
            kb.wait("vector", *b0_loaded[tc])
            kb.step("vector", lambda v, par=par: v.tensor_copy(
                out=csb[:, 1024 * par:1024 * (par + 1)],
                in_=csf[:, 1024 * par:1024 * (par + 1)]), sem="sV")
            cs_cast[tc] = kb.cnt["sV"]
        if tc >= 1:
            tcc = tc - 1
            parc = tcc % 2
            kb.wait("tensor", *b0_loaded[tcc])
            for f in range(6):
                if tcc >= 1:
                    kb.wait("tensor", "sV", evac_f[(tcc - 1, f)])
                for ec in range(16):
                    w_ap = wqkvb[:, (f * 16 + ec) * 128:(f * 16 + ec + 1) * 128]
                    b_ap = b0c[:, (parc * 16 + ec) * 512:(parc * 16 + ec + 1) * 512]
                    kb.step("tensor", lambda t, w=w_ap, b=b_ap, f=f, ec=ec: t.matmul(
                        out=ps_qkv[f][:], lhsT=w, rhs=b,
                        start=(ec == 0), stop=(ec == 15)),
                        sem="sT" if ec == 15 else None)
            mm_done[tcc] = kb.cnt["sT"]
            cs1sl = csb[:, 1024 * parc:1024 * parc + 512]
            cs2sl = csb[:, 1024 * parc + 512:1024 * (parc + 1)]
            mmb = mm_done[tcc] - 6
            for f in range(4):
                hh = f % 2
                dst = Qt if f < 2 else Kt
                dsl = slice(T * hh + 512 * tcc, T * hh + 512 * (tcc + 1))
                zsl = zb[:, 512 * f:512 * (f + 1)]
                kb.wait("vector", "sT", mmb + f + 1)
                evac_f[(tcc, f)] = kb.step(
                    "vector", lambda v, f=f, zsl=zsl: v.tensor_copy(
                        out=zsl, in_=ps_qkv[f][:]), sem="sV")
                kb.step("vector", lambda v, zsl=zsl, cs1sl=cs1sl: v.tensor_tensor(
                    out=rt1[:], in0=zsl, in1=cs1sl, op=ALU.mult))
                kb.step("vector", lambda v, zsl=zsl, cs2sl=cs2sl: v.tensor_tensor(
                    out=rt2[0:64, :], in0=zsl[64:128, :], in1=cs2sl[64:128, :],
                    op=ALU.mult))
                kb.step("vector", lambda v, zsl=zsl, cs2sl=cs2sl: v.tensor_tensor(
                    out=rt2[64:128, :], in0=zsl[0:64, :], in1=cs2sl[0:64, :],
                    op=ALU.mult))
                kb.step("vector", lambda v, dst=dst, dsl=dsl: v.tensor_tensor(
                    out=dst[:, dsl], in0=rt1[:], in1=rt2[:], op=ALU.subtract),
                    sem="sV")
            for f in (4, 5):
                hh = f % 2
                dsl = slice(T * hh + 512 * tcc, T * hh + 512 * (tcc + 1))
                zsl = zb[:, 512 * f:512 * (f + 1)]
                kb.wait("vector", "sT", mmb + f + 1)
                evac_f[(tcc, f)] = kb.step(
                    "vector", lambda v, f=f, zsl=zsl: v.tensor_copy(
                        out=zsl, in_=ps_qkv[f][:]), sem="sV")
                kb.wait("tensor", "sV")
                for j in range(4):
                    kb.step("tensor", lambda t, zsl=zsl, j=j: t.transpose(
                        out=ps_vt[:, 128 * j:128 * (j + 1)],
                        in_=zsl[:, 128 * j:128 * (j + 1)], identity=identb[:]),
                        sem="sT" if j == 3 else None)
                kb.wait("vector", "sT")
                kb.step("vector", lambda v, dsl=dsl: v.tensor_copy(
                    out=Vt[:, dsl], in_=ps_vt[:]), sem="sV")
            evac_done = kb.cnt["sV"]

    kb.barrier()
    es1.close()

    if DEBUG:
        kqv_dbg = {nm: nc.dram_tensor(nm + "_dbg", [128, 2 * T], BF16,
                                      kind="ExternalOutput").ap()
                   for nm in ("Kt", "Qt", "Vt")}
        for nm, tile in (("Kt", Kt), ("Qt", Qt), ("Vt", Vt)):
            kb.dma("gpsimd", kqv_dbg[nm], tile[:], "qG")
        b0_dbg = nc.dram_tensor("b0_dbg", [2 * NC * E, 512], BF16,
                                kind="ExternalOutput").ap()
        for h in range(2):
            kb.dma("gpsimd", b0_dbg[NC * E * h:NC * E * (h + 1), :],
                   cc_b0_out[h], "qG")
        wq_dbg = nc.dram_tensor("wq_dbg", [128, 6 * 16 * 128], BF16,
                                kind="ExternalOutput").ap()
        kb.dma("gpsimd", wq_dbg, wqkvb[:], "qG")
        kb.wait("gpsimd", "qG")
        kb.barrier()

    es2p = ExitStack()
    Pt = es2p.enter_context(nc.sbuf_tensor("Pt", [128, 2 * 1024], BF16))
    Ygrp = es2p.enter_context(nc.sbuf_tensor("Ygrp", [128, 2 * GT], BF16))
    rlB = es2p.enter_context(nc.sbuf_tensor("rlB", [128, 1024], BF16))
    oev = es2p.enter_context(nc.sbuf_tensor("oev", [128, 2 * 512], F32))

    exp_done = {}
    prev_proj_sv = 0
    pend_tail = None
    st_yh = {"t": 0}
    for g in range(NG):
        esg = ExitStack()
        ps_sc = [esg.enter_context(nc.psum_tensor(f"ps_sc{g}_{j}", [128, 1024], F32))
                 for j in range(2)]
        ps_y = [esg.enter_context(nc.psum_tensor(f"ps_y{g}_{j}", [128, 512], F32))
                for j in range(2)]
        ps_l = [esg.enter_context(nc.psum_tensor(f"ps_l{g}_{j}", [128, 512], F32))
                for j in range(2)]
        kb.wait("tensor", "sV", prev_proj_sv)
        for hh in range(2):
            for qcp_in_g in range(2):
                qcp = 2 * g + qcp_in_g
                qc0, qc1 = 2 * qcp, 2 * qcp + 1
                kmax0, kmax1 = 4 * qc0 + 3, 4 * qc1 + 3
                need_v = {}
                for kt in range(kmax1 + 2):
                    if kt <= kmax1:
                        par = kt % 2
                        both = kt <= kmax0
                        prev = exp_done.get((hh, qcp, kt - 2))
                        if prev is None:
                            prev = kb.cnt["sS"] if kt < 2 else None
                        if prev:
                            kb.wait("tensor", "sS", prev)
                        q_ap1 = Qt[:, T * hh + 512 * qc1: T * hh + 512 * (qc1 + 1)]
                        k_ap = Kt[:, T * hh + 128 * kt: T * hh + 128 * (kt + 1)]
                        if both:
                            q_ap0 = Qt[:, T * hh + 512 * qc0: T * hh + 512 * (qc0 + 1)]
                            kb.step("tensor", lambda t, k=k_ap, q=q_ap0, par=par:
                                    t.matmul(out=ps_sc[par][:, 0:512], lhsT=k, rhs=q,
                                             start=True, stop=True))
                        kb.step("tensor", lambda t, k=k_ap, q=q_ap1, par=par:
                                t.matmul(out=ps_sc[par][:, 512:1024], lhsT=k, rhs=q,
                                         start=True, stop=True), sem="sT")
                        kb.wait("scalar", "sT")
                        if both:
                            kb.step("scalar", lambda s, par=par: s.activation(
                                out=Pt[:, 1024 * par:1024 * (par + 1)],
                                in_=ps_sc[par][:], func=AF.Exp, scale=C_SCALE),
                                sem="sS")
                        else:
                            kb.step("scalar", lambda s, par=par: s.activation(
                                out=Pt[:, 1024 * par + 512:1024 * (par + 1)],
                                in_=ps_sc[par][:, 512:1024], func=AF.Exp,
                                scale=C_SCALE), sem="sS")
                        exp_done[(hh, qcp, kt)] = kb.cnt["sS"]
                        nv = 0
                        if both and 4 * qc0 <= kt <= 4 * qc0 + 3:
                            off = 384 - 128 * (kt - 4 * qc0)
                            kb.wait("vector", "sS")
                            kb.step("vector", lambda v, par=par, off=off:
                                    v.tensor_tensor(
                                        out=Pt[:, 1024 * par:1024 * par + 512],
                                        in0=Pt[:, 1024 * par:1024 * par + 512],
                                        in1=maskM[:, off:off + 512], op=ALU.mult),
                                    sem="sV")
                            nv = kb.cnt["sV"]
                        if 4 * qc1 <= kt <= 4 * qc1 + 3:
                            off = 384 - 128 * (kt - 4 * qc1)
                            kb.wait("vector", "sS")
                            kb.step("vector", lambda v, par=par, off=off:
                                    v.tensor_tensor(
                                        out=Pt[:, 1024 * par + 512:1024 * (par + 1)],
                                        in0=Pt[:, 1024 * par + 512:1024 * (par + 1)],
                                        in1=maskM[:, off:off + 512], op=ALU.mult),
                                    sem="sV")
                            nv = kb.cnt["sV"]
                        need_v[kt] = nv
                    if kt == 1 and pend_tail is not None:
                        pend_tail()
                        pend_tail = None
                    if kt >= 1:
                        ky = kt - 1
                        pary = ky % 2
                        bothy = ky <= kmax0
                        if ky == 0 and st_yh["t"]:
                            kb.wait("tensor", "sV", st_yh["t"])
                        kb.wait("tensor", "sS", exp_done[(hh, qcp, ky)])
                        if need_v[ky]:
                            kb.wait("tensor", "sV", need_v[ky])
                        v_ap = Vt[:, T * hh + 128 * ky: T * hh + 128 * (ky + 1)]
                        p0_ap = Pt[:, 1024 * pary:1024 * pary + 512]
                        p1_ap = Pt[:, 1024 * pary + 512:1024 * (pary + 1)]
                        if bothy:
                            kb.step("tensor", lambda t, v=v_ap, p=p0_ap, ky=ky,
                                    k0=kmax0: t.matmul(out=ps_y[0][:], lhsT=v,
                                                       rhs=p, start=(ky == 0),
                                                       stop=(ky == k0)))
                            kb.step("tensor", lambda t, p=p0_ap, ky=ky, k0=kmax0:
                                    t.matmul(out=ps_l[0][0:1, :], lhsT=ones128[:],
                                             rhs=p, start=(ky == 0),
                                             stop=(ky == k0)))
                        kb.step("tensor", lambda t, v=v_ap, p=p1_ap, ky=ky,
                                k1=kmax1: t.matmul(out=ps_y[1][:], lhsT=v, rhs=p,
                                                   start=(ky == 0),
                                                   stop=(ky == k1)))
                        kb.step("tensor", lambda t, p=p1_ap, ky=ky, k1=kmax1:
                                t.matmul(out=ps_l[1][0:1, :], lhsT=ones128[:],
                                         rhs=p, start=(ky == 0), stop=(ky == k1)),
                                sem="sT" if ky == kmax1 else None)
                kb.wait("vector", "sT")
                kb.step("vector", lambda v: v.reciprocal(
                    out=rw_m[0:1, 0:512], in_=ps_l[0][0:1, :]))
                kb.step("vector", lambda v: v.reciprocal(
                    out=rw_v[0:1, 0:512], in_=ps_l[1][0:1, :]))
                kb.step("vector", lambda v: v.tensor_copy(
                    out=rw_mb[0:1, 0:512], in_=rw_m[0:1, 0:512]))
                kb.step("vector", lambda v: v.tensor_copy(
                    out=rw_rb[0:1, 0:512], in_=rw_v[0:1, 0:512]), sem="sV")
                vA = kb.cnt["sV"]

                def mk_tail(hh=hh, qc0=qc0, qc1=qc1, vA=vA, g=g,
                            ps_l=ps_l, ps_y=ps_y, Ygrp=Ygrp):
                    def tail():
                        kb.wait("tensor", "sV", vA)
                        kb.step("tensor", lambda t: t.matmul(
                            out=ps_l[0][:], lhsT=ones1[:],
                            rhs=rw_mb[0:1, 0:512], start=True, stop=True))
                        kb.step("tensor", lambda t: t.matmul(
                            out=ps_l[1][:], lhsT=ones1[:],
                            rhs=rw_rb[0:1, 0:512], start=True, stop=True),
                            sem="sT")
                        bc = kb.cnt["sT"]
                        kb.wait("vector", "sT", bc)
                        for j in range(2):
                            kb.step("vector", lambda v, j=j: v.tensor_copy(
                                out=rlB[:, 512 * j:512 * (j + 1)],
                                in_=ps_l[j][:]))
                        for j, qc in ((0, qc0), (1, qc1)):
                            ysl = Ygrp[:, GT * hh + 512 * (qc - 4 * g):
                                       GT * hh + 512 * (qc - 4 * g + 1)]
                            kb.step("vector", lambda v, j=j, ysl=ysl:
                                    v.tensor_tensor(out=ysl, in0=ps_y[j][:],
                                                    in1=rlB[:, 512 * j:
                                                            512 * (j + 1)],
                                                    op=ALU.mult), sem="sV")
                        st_yh["t"] = kb.cnt["sV"]
                    return tail
                pend_tail = mk_tail()
        pend_tail()
        pend_tail = None
        esg.close()
        esp = ExitStack()
        ps_o = [esp.enter_context(nc.psum_tensor(f"ps_o{g}_{j}", [128, 512], F32))
                for j in range(2)]
        proj_sv_base = kb.cnt["sV"]
        kb.wait("tensor", "sV")
        kb.wait("vector", "qG")
        kb.wait("vector", "qH")
        nproj = 0
        for tch in range(4):
            for fc in range(16):
                j = nproj % 2
                if nproj >= 2:
                    kb.wait("tensor", "sV", proj_sv_base + nproj - 1)
                for e2 in range(2):
                    w_ap = wpb[:, (e2 * 16 + fc) * 128:(e2 * 16 + fc + 1) * 128]
                    y_ap = Ygrp[:, GT * e2 + 512 * tch: GT * e2 + 512 * (tch + 1)]
                    kb.step("tensor", lambda t, w=w_ap, y=y_ap, j=j, e2=e2:
                            t.matmul(out=ps_o[j][:], lhsT=w, rhs=y,
                                     start=(e2 == 0), stop=(e2 == 1)),
                            sem="sT" if e2 == 1 else None)
                kb.wait("vector", "sT")
                stsem = "qG" if j == 0 else "qH"
                if nproj >= 2:
                    kb.wait("vector", stsem)
                kb.step("vector", lambda v, j=j: v.tensor_copy(
                    out=oev[:, 512 * j:512 * (j + 1)], in_=ps_o[j][:]), sem="sV")
                kb.wait("gpsimd", "sV")
                for rr in range(2):
                    p = 2 * tch + rr
                    kb.dma("gpsimd",
                           rs_in[g][E * p + 128 * fc: E * p + 128 * (fc + 1), :],
                           oev[:, 512 * j + 256 * rr: 512 * j + 256 * (rr + 1)],
                           stsem)
                nproj += 1
        prev_proj_sv = kb.cnt["sV"]
        kb.wait("gpsimd", "qG")
        kb.wait("gpsimd", "qH")
        if DEBUG:
            rsin_dbg = nc.dram_tensor(f"rsin_dbg{g}", [NC * E, PC], F32,
                                      kind="ExternalOutput").ap()
            kb.dma("gpsimd", rsin_dbg, rs_in[g], "qG")
            kb.wait("gpsimd", "qG")
        kb.step("gpsimd", lambda gg, g=g: gg.collective_compute(
            "ReduceScatter", ALU.add, ins=[rs_in[g]], outs=[rs_out[g]],
            replica_groups=[list(range(NC))]), sem="sCC")
        esp.close()

    sync_fence()
    es2p.close()
    es12.close()
    es_w.close()

    es34 = ExitStack()
    b_ffn = es34.enter_context(nc.sbuf_tensor("b_ffn", [128, 16 * TPC], BF16))

    es3 = ExitStack()
    x0r = es3.enter_context(nc.sbuf_tensor("x0r", [128, 16 * TPC], BF16))
    x1bs = es3.enter_context(nc.sbuf_tensor("x1bs", [128, 16 * TPC], BF16))
    oBs = es3.enter_context(nc.sbuf_tensor("oBs", [128, 16 * 256], BF16))
    oF = es3.enter_context(nc.sbuf_tensor("oF", [128, 2 * 4096], F32))
    sqst3 = es3.enter_context(nc.sbuf_tensor("sqst3", [128, 2048], BF16))
    mB3 = es3.enter_context(nc.sbuf_tensor("mB3", [128, 512], BF16))
    rB3 = es3.enter_context(nc.sbuf_tensor("rB3", [128, 512], BF16))
    ps_s3 = es3.enter_context(nc.psum_tensor("ps_s3", [1, 512], F32))
    ps_s23 = es3.enter_context(nc.psum_tensor("ps_s23", [1, 512], F32))
    ps_bc3 = es3.enter_context(nc.psum_tensor("ps_bc3", [128, 512], F32))

    for ec in range(16):
        kb.dma("sync", x0r[:, TPC * ec:TPC * (ec + 1)],
               x0b_sp[128 * ec:128 * (ec + 1), :], "qC")
    sCC_rs0 = kb.cnt["sCC"] - NG
    if DEBUG:
        rs_dbg = nc.dram_tensor("rs_dbg", [E, PC], F32, kind="ExternalOutput").ap()
        kb.wait("gpsimd", "sCC", sCC_rs0 + 1)
        kb.dma("gpsimd", rs_dbg, rs_out[0], "qG")
    cast_done = {}
    for g in range(NG):
        slot = g % 2
        ldsem3 = "qD" if slot == 0 else "qE"
        kb.wait("sync", "sCC", sCC_rs0 + 1 + g)
        if g >= 2:
            kb.wait("sync", "sV", cast_done[g - 2])
        for ec in range(16):
            kb.dma("sync", oF[:, 4096 * slot + 256 * ec:
                              4096 * slot + 256 * (ec + 1)],
                   rs_out[g][128 * ec:128 * (ec + 1), :], ldsem3)
        kb.wait("vector", ldsem3)
        for ec in range(16):
            kb.step("vector", lambda v, ec=ec, slot=slot: v.tensor_copy(
                out=oBs[:, 256 * ec:256 * (ec + 1)],
                in_=oF[:, 4096 * slot + 256 * ec:4096 * slot + 256 * (ec + 1)]),
                sem="sV")
        cast_done[g] = kb.cnt["sV"]
        strips = [oBs[:, 256 * ec:256 * (ec + 1)] for ec in range(16)]
        ln_stats(strips, 256, ps_s3, ps_s23, ps_bc3, sqst3, mB3, rB3)
        kb.wait("vector", "qC")
        for ec in range(16):
            osl = oBs[:, 256 * ec:256 * (ec + 1)]
            x0sl = x0r[:, TPC * ec + 256 * g: TPC * ec + 256 * (g + 1)]
            x1sl = x1bs[:, TPC * ec + 256 * g: TPC * ec + 256 * (g + 1)]
            kb.step("vector", lambda v, osl=osl: v.tensor_tensor(
                out=osl, in0=osl, in1=mB3[:, 0:256], op=ALU.subtract))
            kb.step("vector", lambda v, osl=osl: v.tensor_tensor(
                out=osl, in0=osl, in1=rB3[:, 0:256], op=ALU.mult))
            kb.step("vector", lambda v, osl=osl, x0sl=x0sl, x1sl=x1sl:
                    v.tensor_tensor(out=x1sl, in0=osl, in1=x0sl, op=ALU.add),
                    sem="sV")

    kb.wait("gpsimd", "sV")
    for ec in range(16):
        kb.dma("gpsimd", x1b_sp[128 * ec:128 * (ec + 1), :],
               x1bs[:, TPC * ec:TPC * (ec + 1)], "qG")
    for tb in range(2):
        strips = [x1bs[:, TPC * ec + 512 * tb: TPC * ec + 512 * (tb + 1)]
                  for ec in range(16)]
        ln_stats(strips, 512, ps_s3, ps_s23, ps_bc3, sqst3, mB3, rB3)
        for ec in range(16):
            src = strips[ec]
            dst = b_ffn[:, TPC * ec + 512 * tb: TPC * ec + 512 * (tb + 1)]
            kb.step("vector", lambda v, src=src, dst=dst: v.tensor_tensor(
                out=dst, in0=src, in1=mB3[:], op=ALU.subtract))
            kb.step("vector", lambda v, dst=dst: v.tensor_tensor(
                out=dst, in0=dst, in1=rB3[:], op=ALU.mult), sem="sV")

    sync_fence()
    es3.close()

    if DEBUG:
        bffn_dbg = nc.dram_tensor("bffn_dbg", [E, TPC], BF16,
                                  kind="ExternalOutput").ap()
        kb.wait("gpsimd", "sV")
        for ec in range(16):
            kb.dma("gpsimd", bffn_dbg[128 * ec:128 * (ec + 1), :],
                   b_ffn[:, TPC * ec:TPC * (ec + 1)], "qG")
        kb.wait("gpsimd", "qG")
        kb.barrier()

    FW = 512
    FWINS = MLP // FW
    es4 = ExitStack()
    gwin = es4.enter_context(nc.sbuf_tensor("gwin", [128, 8 * TPC], BF16))
    es4a = ExitStack()
    w4buf = [es4a.enter_context(nc.sbuf_tensor(f"w4buf{s}", [128, 16384], U16))
             for s in range(2)]
    fstage = es4a.enter_context(nc.sbuf_tensor("fstage", [128, 2 * 512], F32))
    pkbuf = es4a.enter_context(nc.sbuf_tensor("pkbuf", [128, 2 * 4096], U16))
    s16buf = es4a.enter_context(nc.sbuf_tensor("s16buf", [128, 2 * 4096], BF16))
    scf = es4a.enter_context(nc.sbuf_tensor("scf", [128, 4096], F32))
    dq1 = es4a.enter_context(nc.sbuf_tensor("dq1", [128, 2048], U16))
    dq2 = es4a.enter_context(nc.sbuf_tensor("dq2", [128, 2048], U16))
    ps_gu = [es4a.enter_context(nc.psum_tensor(f"ps_gu{j}", [128, 512], F32))
             for j in range(4)]
    ps_dn = [es4a.enter_context(nc.psum_tensor(f"ps_dn{j}", [128, 512], F32))
             for j in range(4)]


    phases = []
    for w in range(FWINS):
        phases.append(("g", w))
        phases.append(("u", w))
        if w % 2 == 1:
            phases.append(("d", w // 2))

    def emit_phase_loads(kind, w, slot):
        ldsem = "qL" if slot == 0 else "qM"
        scsem = "qC" if slot == 0 else "qF"
        if kind in ("g", "u"):
            pk_ = pkg_d if kind == "g" else pku_d
            sc_ = scg_d if kind == "g" else scu_d
            for c in range(4):
                kb.dma("sync", pkbuf[:, 4096 * slot + 512 * c:
                                     4096 * slot + 512 * (c + 1)],
                       pk_[c, :, FW * w:FW * (w + 1)], ldsem)
                srcap = sc_[16 * c:16 * (c + 1), FW * w:FW * (w + 1)]
                srcap = srcap.unsqueeze(1).broadcast_to([16, 8, FW])
                kb.dma("sync", scf[:, 512 * c:512 * (c + 1)], srcap, scsem)
        else:
            for cc in range(2):
                c = 2 * w + cc
                kb.dma("sync", pkbuf[:, 4096 * slot + 2048 * cc:
                                     4096 * slot + 2048 * (cc + 1)],
                       pkd_d[c], ldsem)
                srcap = scd_d[16 * c:16 * (c + 1), :]
                srcap = srcap.unsqueeze(1).broadcast_to([16, 8, E])
                kb.dma("sync", scf[:, 2048 * cc:2048 * (cc + 1)], srcap, scsem)
        return (ldsem, kb.cnt[ldsem], scsem, kb.cnt[scsem])

    load_tgt = {0: emit_phase_loads(*phases[0], slot=0)}

    SH_S = (12, 8, 4, 0)
    SH_M = (9, 5, 1, -3)

    SH_S2 = (12, 8, 4, 0)

    def make_jobs(pi):
        kind, w = phases[pi]
        slot = pi % 2
        nch, width = (4, FW) if kind in ("g", "u") else (2, 2048)
        tiles = [(c, i) for c in range(nch) for i in range(4)]
        jobs = []
        for idx, (c, i) in enumerate(tiles):
            def job(idx=idx, c=c, i=i, slot=slot, width=width):
                tl = (4 * c + i) * width
                pk_ap = pkbuf[:, 4096 * slot + width * c:
                              4096 * slot + width * (c + 1)]
                s16_ap = s16buf[:, 4096 * slot + width * c:
                                4096 * slot + width * (c + 1)]
                d1 = dq1[:, 0:width]
                d2 = dq2[:, 0:width]
                sh_s, sh_m = SH_S[i], SH_M[i]
                kb.step("vector", lambda v, pk=pk_ap, sh=sh_s, d1=d1:
                        v.tensor_scalar(out=d1, in0=pk, scalar1=sh,
                                        scalar2=0x8000,
                                        op0=ALU.logical_shift_left,
                                        op1=ALU.bitwise_and))
                op = (ALU.logical_shift_left if sh_m >= 0
                      else ALU.logical_shift_right)
                kb.step("vector", lambda v, pk=pk_ap, sh=abs(sh_m), op=op, d2=d2:
                        v.tensor_scalar(out=d2, in0=pk, scalar1=sh,
                                        scalar2=0x0E00, op0=op,
                                        op1=ALU.bitwise_and))
                kb.step("vector", lambda v, tl=tl, d1=d1, d2=d2, slot=slot,
                        width=width: v.tensor_tensor(
                            out=w4buf[slot][:, tl:tl + width], in0=d1, in1=d2,
                            op=ALU.bitwise_or))
                kb.step("vector", lambda v, tl=tl, s16=s16_ap, slot=slot,
                        width=width: v.tensor_tensor(
                            out=w4buf[slot][:, tl:tl + width].bitcast(BF16),
                            in0=w4buf[slot][:, tl:tl + width].bitcast(FP16),
                            in1=s16, op=ALU.mult), sem="sV")
            jobs.append(job)
        return jobs

    from collections import deque
    mm_tgt = {}
    dq_done = {}
    queued = {}
    dq_fin = set()

    mult_tgt = {}

    def queue_phase(p):
        if p in queued or p in dq_fin or p >= len(phases):
            return
        ldsem, ldc, scsem, scc = load_tgt[p]
        slot = p % 2
        if p >= 2:
            kb.wait("scalar", "sV", dq_done[p - 2])
        kb.wait("scalar", scsem, scc)
        mult_tgt[p] = kb.step("scalar", lambda s, slot=slot: s.activation(
            out=s16buf[:, 4096 * slot:4096 * (slot + 1)], in_=scf[:, 0:4096],
            func=AF.Copy, scale=16384.0), sem="sS")
        kb.wait("vector", ldsem, ldc)
        kb.wait("vector", "sS", mult_tgt[p])
        if p >= 2:
            kb.wait("vector", "sT", mm_tgt[p - 2])
        queued[p] = deque(make_jobs(p))

    queue_phase(0)
    kb.wait("sync", "sS", mult_tgt[0])
    load_tgt[1] = emit_phase_loads(*phases[1], slot=1)

    def pop_jobs(p, n):
        q = queued.get(p)
        while q and n > 0:
            q.popleft()()
            n -= 1
        if q is not None and not q:
            dq_done[p] = kb.cnt["sV"]
            dq_fin.add(p)
            del queued[p]

    prev_gu = None
    prev_d = None
    for pi, (kind, w) in enumerate(phases):
        slot = pi % 2
        queue_phase(pi)
        pop_jobs(pi, 99)
        kb.wait("tensor", "sV", dq_done[pi])
        queue_phase(pi + 1)
        if pi + 2 < len(phases):
            kb.wait("sync", "sV", dq_done[pi])
            kb.wait("sync", "sS", mult_tgt[pi + 1])
            load_tgt[pi + 2] = emit_phase_loads(*phases[pi + 2], slot=(pi + 2) % 2)
        if kind in ("g", "u"):
            if kind == "u":
                kb.wait("vector", "sS")
            evsem = "sS" if kind == "g" else "sV"
            evac_tgt = {}
            gi = 0
            for nt in range(4):
                for tb in range(2):
                    j = gi % 4
                    if gi >= 4:
                        kb.wait("tensor", evsem, evac_tgt[gi - 4])
                    elif prev_gu is not None:
                        pv_evac, pv_sem, pv_n = prev_gu
                        kb.wait("tensor", pv_sem, pv_evac[pv_n - 4 + gi])
                    for ci in range(16):
                        w_ap = w4buf[slot][:, ci * FW + 128 * nt:
                                           ci * FW + 128 * (nt + 1)].bitcast(BF16)
                        b_ap = b_ffn[:, TPC * ci + 512 * tb:
                                     TPC * ci + 512 * (tb + 1)]
                        kb.step("tensor", lambda t, wv=w_ap, b=b_ap, ci=ci, j=j:
                                t.matmul(out=ps_gu[j][:], lhsT=wv, rhs=b,
                                         start=(ci == 0), stop=(ci == 15)),
                                sem="sT" if ci == 15 else None)
                    gst = 4 * (w % 2) + nt
                    gsl = gwin[:, TPC * gst + 512 * tb: TPC * gst + 512 * (tb + 1)]
                    if kind == "g":
                        kb.wait("scalar", "sT")
                        evac_tgt[gi] = kb.step(
                            "scalar", lambda s, gsl=gsl, j=j: s.activation(
                                out=gsl, in_=ps_gu[j][:],
                                func=AF.Gelu_apprx_tanh), sem="sS")
                    else:
                        kb.wait("vector", "sT")
                        evac_tgt[gi] = kb.step(
                            "vector", lambda v, gsl=gsl, j=j: v.tensor_tensor(
                                out=gsl, in0=gsl, in1=ps_gu[j][:], op=ALU.mult),
                            sem="sV")
                    gi += 1
                    pop_jobs(pi + 1, 3)
            prev_gu = (evac_tgt, evsem, 8)
        else:
            evac_tgt = {}
            gi = 0
            for hc in range(16):
                for tb in range(2):
                    j = gi % 4
                    if gi >= 4:
                        kb.wait("tensor", "sV", evac_tgt[gi - 4])
                    elif prev_d is not None:
                        pv_evac, pv_sem, pv_n = prev_d
                        kb.wait("tensor", pv_sem, pv_evac[pv_n - 4 + gi])
                    for sub in range(8):
                        w_ap = w4buf[slot][:, sub * E + 128 * hc:
                                           sub * E + 128 * (hc + 1)].bitcast(BF16)
                        h_ap = gwin[:, TPC * sub + 512 * tb:
                                    TPC * sub + 512 * (tb + 1)]
                        kb.step("tensor", lambda t, wv=w_ap, h=h_ap, sub=sub, j=j:
                                t.matmul(out=ps_dn[j][:], lhsT=wv, rhs=h,
                                         start=(sub == 0), stop=(sub == 7)),
                                sem="sT" if sub == 7 else None)
                    kb.wait("vector", "sT")
                    fpar = gi % 2
                    stsem = "qG" if fpar == 0 else "qH"
                    kb.wait("vector", stsem)
                    evac_tgt[gi] = kb.step(
                        "vector", lambda v, fpar=fpar, j=j: v.tensor_copy(
                            out=fstage[:, 512 * fpar:512 * (fpar + 1)],
                            in_=ps_dn[j][:]), sem="sV")
                    kb.wait("gpsimd", "sV", evac_tgt[gi])
                    dst = acc_dram[128 * hc:128 * (hc + 1),
                                   512 * tb:512 * (tb + 1)]
                    if w == 0:
                        kb.step("gpsimd", lambda g2, dst=dst, fpar=fpar:
                                g2.dma_start(out=dst,
                                             in_=fstage[:, 512 * fpar:512 * (fpar + 1)]),
                                sem=stsem, k=16)
                    else:
                        kb.step("gpsimd", lambda g2, dst=dst, fpar=fpar:
                                g2.dma_start(out=dst,
                                             in_=fstage[:, 512 * fpar:512 * (fpar + 1)],
                                             accum_op=ALU.add), sem=stsem, k=16)
                    gi += 1
                    pop_jobs(pi + 1, 2 if gi <= 16 else 1)
            prev_d = (evac_tgt, "sV", 32)
        mm_tgt[pi] = kb.cnt["sT"]

    sync_fence()
    es4a.close()

    if DEBUG:
        acc_dbg = nc.dram_tensor("acc_dbg", [E, TPC], F32,
                                 kind="ExternalOutput").ap()
        kb.wait("gpsimd", "qG")
        kb.wait("gpsimd", "qH")
        kb.dma("gpsimd", acc_dbg, acc_dram, "qG")
        kb.wait("gpsimd", "qG")
        kb.barrier()

    acc_sb = es4.enter_context(nc.sbuf_tensor("acc_sb", [128, 16 * 512], F32))
    x1r = es4.enter_context(nc.sbuf_tensor("x1r", [128, 16 * 512], BF16))
    aBs = es4.enter_context(nc.sbuf_tensor("aBs", [128, 16 * 512], BF16))
    sq5 = es4.enter_context(nc.sbuf_tensor("sq5", [128, 16 * 512], BF16))
    mB5 = es4.enter_context(nc.sbuf_tensor("mB5", [128, 512], F32))
    rB5 = es4.enter_context(nc.sbuf_tensor("rB5", [128, 512], F32))
    ot1 = es4.enter_context(nc.sbuf_tensor("ot1", [128, 2 * 512], F32))
    ps_s5 = es4.enter_context(nc.psum_tensor("ps_s5", [1, 512], F32))
    ps_s25 = es4.enter_context(nc.psum_tensor("ps_s25", [1, 512], F32))
    ps_bc5 = es4.enter_context(nc.psum_tensor("ps_bc5", [128, 512], F32))

    for tb in range(2):
        if tb == 0:
            kb.wait("sync", "qG")
            kb.wait("sync", "qH")
        else:
            kb.wait("sync", "sV")
        x1sem = "qD" if tb == 0 else "qE"
        for ec in range(16):
            ldsem5 = "qL" if (ec % 2) == 0 else "qM"
            kb.dma("sync", acc_sb[:, 512 * ec:512 * (ec + 1)],
                   acc_dram[128 * ec:128 * (ec + 1), 512 * tb:512 * (tb + 1)],
                   ldsem5)
        for ec in range(16):
            kb.dma("sync", x1r[:, 512 * ec:512 * (ec + 1)],
                   x1b_sp[128 * ec:128 * (ec + 1), 512 * tb:512 * (tb + 1)],
                   x1sem)
        kb.wait("vector", "qL")
        kb.wait("vector", "qM")
        kb.wait("vector", "sT")
        cp_tgt = {}
        for ec in range(16):
            kb.step("vector", lambda v, ec=ec: v.tensor_copy(
                out=aBs[:, 512 * ec:512 * (ec + 1)],
                in_=acc_sb[:, 512 * ec:512 * (ec + 1)]), sem="sV")
            cp_tgt[ec] = kb.cnt["sV"]
        kb.wait("scalar", "sT")
        sq_tgt = {}
        for ec in range(16):
            kb.wait("scalar", "sV", cp_tgt[ec])
            kb.step("scalar", lambda s, ec=ec: s.activation(
                out=sq5[:, 512 * ec:512 * (ec + 1)],
                in_=aBs[:, 512 * ec:512 * (ec + 1)],
                func=AF.Square), sem="sS")
            sq_tgt[ec] = kb.cnt["sS"]
        for ec in range(16):
            kb.wait("tensor", "sV", cp_tgt[ec])
            kb.step("tensor", lambda t, ec=ec: t.matmul(
                out=ps_s5[0:1, :], lhsT=ones128[:],
                rhs=aBs[:, 512 * ec:512 * (ec + 1)],
                start=(ec == 0), stop=(ec == 15)))
            kb.wait("tensor", "sS", sq_tgt[ec])
            kb.step("tensor", lambda t, ec=ec: t.matmul(
                out=ps_s25[0:1, :], lhsT=ones128[:],
                rhs=sq5[:, 512 * ec:512 * (ec + 1)],
                start=(ec == 0), stop=(ec == 15)), sem="sT")
        kb.wait("vector", "sT")
        kb.step("vector", lambda v: v.tensor_scalar_mul(
            rw_m[0:1, 0:512], ps_s5[0:1, :], R_E))
        kb.step("vector", lambda v: v.tensor_tensor(
            out=rw_t[0:1, 0:512], in0=rw_m[0:1, 0:512], in1=rw_m[0:1, 0:512],
            op=ALU.mult))
        kb.step("vector", lambda v: v.scalar_tensor_tensor(
            out=rw_v[0:1, 0:512], in0=ps_s25[0:1, :], scalar=R_E,
            in1=rw_t[0:1, 0:512], op0=ALU.mult, op1=ALU.subtract), sem="sV")
        kb.wait("scalar", "sV")
        kb.step("scalar", lambda s: s.activation(
            out=rw_v[0:1, 0:512], in_=rw_v[0:1, 0:512], func=AF.Sqrt, bias=epsb[0:1, 0:1]),
            sem="sS")
        kb.wait("vector", "sS")
        kb.step("vector", lambda v: v.reciprocal(
            out=rw_v[0:1, 0:512], in_=rw_v[0:1, 0:512]), sem="sV")
        kb.wait("tensor", "sV")
        kb.step("tensor", lambda t: t.matmul(
            out=ps_bc5[:], lhsT=ones1f[:], rhs=rw_m[0:1, 0:512],
            start=True, stop=True), sem="sT")
        kb.wait("vector", "sT")
        kb.step("vector", lambda v: v.tensor_copy(out=mB5[:], in_=ps_bc5[:]),
                sem="sV")
        kb.wait("tensor", "sV")
        kb.step("tensor", lambda t: t.matmul(
            out=ps_bc5[:], lhsT=ones1f[:], rhs=rw_v[0:1, 0:512],
            start=True, stop=True), sem="sT")
        kb.wait("vector", "sT")
        kb.step("vector", lambda v: v.tensor_copy(out=rB5[:], in_=ps_bc5[:]),
                sem="sV")
        kb.wait("vector", x1sem)
        kb.wait("vector", "qG")
        kb.wait("vector", "qH")
        for ec in range(16):
            slot = 512 * (ec % 2)
            asl = acc_sb[:, 512 * ec:512 * (ec + 1)]
            if ec >= 2:
                stsem_p = "qG" if (ec % 2) == 0 else "qH"
                kb.wait("vector", stsem_p)
            kb.step("vector", lambda v, asl=asl, slot=slot: v.tensor_tensor(
                out=ot1[:, slot:slot + 512], in0=asl, in1=mB5[:],
                op=ALU.subtract))
            kb.step("vector", lambda v, slot=slot: v.tensor_tensor(
                out=ot1[:, slot:slot + 512], in0=ot1[:, slot:slot + 512],
                in1=rB5[:], op=ALU.mult))
            kb.step("vector", lambda v, slot=slot, ec=ec: v.tensor_tensor(
                out=ot1[:, slot:slot + 512], in0=ot1[:, slot:slot + 512],
                in1=x1r[:, 512 * ec:512 * (ec + 1)], op=ALU.add), sem="sV")
            stsem = "qG" if (ec % 2) == 0 else "qH"
            kb.wait("gpsimd", "sV")
            kb.dma("gpsimd",
                   out_d[128 * ec:128 * (ec + 1), 512 * tb:512 * (tb + 1)],
                   ot1[:, slot:slot + 512], stsem)

    kb.wait("gpsimd", "qG")
    kb.wait("gpsimd", "qH")

    with nc.Block() as block:
        block.sync(lambda e: [f(e) for f in kb.prog["sync"]])
        block.vector(lambda e: [f(e) for f in kb.prog["vector"]])
        block.scalar(lambda e: [f(e) for f in kb.prog["scalar"]])
        block.tensor(lambda e: [f(e) for f in kb.prog["tensor"]])
        block.gpsimd(lambda e: [f(e) for f in kb.prog["gpsimd"]])
    es4.close()
    es34.close()
    es.close()
    return nc



def kernel(**inputs):
    d = _hostprep(inputs)
    nc = build_program()
    in_maps = []
    for r in range(NC):
        in_maps.append(dict(
            x0T_own=d["x0T_own"][r].astype(np.float32),
            wqkvT=d["wqkvT"][r].astype(np.float32),
            wpT_own=d["wpT_own"][r].astype(np.float32),
            pk_g=d["pk_g"], pk_u=d["pk_u"], pk_d=d["pk_d"],
            scT_g=d["scT_g"].astype(np.float32),
            scT_u=d["scT_u"].astype(np.float32),
            scT_d=d["scT_d"].astype(np.float32),
            CS1=d["CS1"], CS2=d["CS2"],
            maskM=d["maskM"], ident=d["ident"],
        ))
    res = run_bass_kernel_spmd(nc, in_maps, core_ids=list(range(NC)))
    big = np.empty((E, T), dtype=np.float32)
    for r in range(NC):
        big[:, d["own_cols"][r]] = res.results[r]["out"]
    out = np.empty((T, E), dtype=np.float32)
    out[:, PI_E] = big.T
    return out[None]


# version 16
# speedup vs baseline: 1.0218x; 1.0063x over previous
import numpy as np
import ml_dtypes
from contextlib import ExitStack

import concourse.bass as bass
import concourse.mybir as mybir
from concourse.bass_utils import run_bass_kernel_spmd

F32 = mybir.dt.float32
BF16 = mybir.dt.bfloat16
FP16 = mybir.dt.float16
U16 = mybir.dt.uint16
AF = mybir.ActivationFunctionType
ALU = mybir.AluOpType
NPBF16 = ml_dtypes.bfloat16

E = 2048
T = 8192
H = 16
D = 128
MLP = 16384
NC = 8
EPS = 1e-5
TPC = T // NC
NG = 4
GT = T // NG
PC = GT // NC
NWIN = 1024
NWINS = MLP // NWIN
C_SCALE = float(np.float32(1.0) / np.sqrt(np.float32(128.0)))
R_E = float(np.float32(1.0 / E))

ENGS = ("sync", "vector", "scalar", "tensor", "gpsimd")
DEBUG = False



def _perm_enum(total):
    out = np.empty(total, dtype=np.int64)
    for c in range(total // 512):
        for i in range(4):
            for p in range(128):
                out[512 * c + 128 * i + p] = 512 * c + 4 * p + i
    return out


PI_E = _perm_enum(E)
PI_N = _perm_enum(MLP)
PI_D = np.concatenate([np.arange(0, 128, 2), np.arange(1, 128, 2)])


def _pack_tiles(mat, rt, ct):
    R, C = mat.shape
    return np.ascontiguousarray(
        mat.reshape(R // rt, rt, C // ct, ct).transpose(0, 2, 1, 3))


def _prep_packed(W_packed_int32, row_perm):
    P8 = np.asarray(W_packed_int32).astype(np.uint8)[row_perm]
    U = P8.view(np.uint16)
    Tt = np.ascontiguousarray(U.T)
    K4 = Tt.shape[0]
    return np.ascontiguousarray(Tt.reshape(K4 // 128, 128, -1))


def _hostprep(inp):
    d = {}
    x = np.asarray(inp["L0_attn_x"])[0]
    x0T = np.ascontiguousarray(x.T)[PI_E]
    own_cols = [np.concatenate([np.arange(GT * g + PC * r, GT * g + PC * (r + 1))
                                for g in range(NG)]) for r in range(NC)]
    d["own_cols"] = own_cols
    d["x0T_own"] = [np.ascontiguousarray(x0T[:, own_cols[r]]) for r in range(NC)]

    Wqkv = np.asarray(inp["L0_attn_W_qkv"])
    Wq = Wqkv[0:E].reshape(H, D, E)[:, PI_D].reshape(E, E)
    Wk = Wqkv[E:2 * E].reshape(H, D, E)[:, PI_D].reshape(E, E)
    Wv = Wqkv[2 * E:3 * E]
    d["wqkvT"] = []
    for r in range(NC):
        Wf = np.concatenate([W[D * 2 * r: D * (2 * r + 2)] for W in (Wq, Wk, Wv)], 0)
        WfT = np.ascontiguousarray(Wf[:, PI_E].T)
        tl = _pack_tiles(WfT, 128, 128)
        d["wqkvT"].append(np.ascontiguousarray(
            tl.transpose(0, 2, 1, 3).reshape(16, 128, 6 * 128)))

    Wp = np.asarray(inp["L0_attn_W_proj"])[PI_E]
    wpT = _pack_tiles(np.ascontiguousarray(Wp.T), 128, 128)
    d["wpT_own"] = [np.ascontiguousarray(
        wpT[2 * r:2 * r + 2].transpose(0, 2, 1, 3).reshape(2, 128, 16 * 128))
        for r in range(NC)]

    d["pk_g"] = _prep_packed(inp["L0_ffn_gate_W_packed"], PI_N)
    d["pk_u"] = _prep_packed(inp["L0_ffn_up_W_packed"], PI_N)
    d["pk_d"] = _prep_packed(inp["L0_ffn_down_W_packed"], PI_E)
    d["scT_g"] = np.ascontiguousarray(np.asarray(inp["L0_ffn_gate_scales"])[PI_N].T)
    d["scT_u"] = np.ascontiguousarray(np.asarray(inp["L0_ffn_up_scales"])[PI_N].T)
    d["scT_d"] = np.ascontiguousarray(np.asarray(inp["L0_ffn_down_scales"])[PI_E].T)

    inv = 1.0 / (10000.0 ** (np.arange(0, D, 2, dtype=np.float64) / D))
    ang = np.outer(inv, np.arange(T, dtype=np.float64))
    cos = np.cos(ang).astype(np.float32)
    sin = np.sin(ang).astype(np.float32)
    d["CS1"] = np.ascontiguousarray(np.concatenate([cos, cos], 0))
    d["CS2"] = np.ascontiguousarray(np.concatenate([-sin, sin], 0))

    cgrid = np.arange(896)[None, :]
    igrid = np.arange(128)[:, None]
    d["maskM"] = (cgrid >= igrid + 384).astype(NPBF16)
    d["ident"] = np.eye(128, dtype=NPBF16)
    return d



class KB:
    def __init__(self, nc, es):
        self.nc = nc
        self.es = es
        self.prog = {e: [] for e in ENGS}
        self.sems = {}
        self.cnt = {}

    def sem(self, *names):
        for n in names:
            self.sems[n] = self.es.enter_context(self.nc.semaphore(n))
            self.cnt[n] = 0

    def step(self, eng, fn, sem=None, k=1):
        if sem is not None:
            self.cnt[sem] += k
            tgt = self.cnt[sem]
            s = self.sems[sem]

            def go(e, fn=fn, s=s, k=k):
                fn(e).then_inc(s, k)
            self.prog[eng].append(go)
            return tgt
        self.prog[eng].append(lambda e, fn=fn: fn(e))
        return None

    def dma(self, eng, out, in_, sem):
        return self.step(eng, lambda e, o=out, i=in_: e.dma_start(out=o, in_=i),
                         sem=sem, k=16)

    def wait(self, eng, sem, val=None):
        v = self.cnt[sem] if val is None else val
        if v <= 0:
            return
        s = self.sems[sem]
        self.prog[eng].append(lambda e, s=s, v=v: e.wait_ge(s, v))

    def barrier(self):
        for e in ENGS:
            self.step(e, lambda eng: eng.nop(), sem="b_" + e, k=1)
        snap = {e: self.cnt["b_" + e] for e in ENGS}
        for e in ENGS:
            for o in ENGS:
                if o != e:
                    self.wait(e, "b_" + o, snap[o])



def build_program():
    nc = bass.Bass("TRN2", target_bir_lowering=False)
    es = ExitStack()
    kb = KB(nc, es)
    kb.sem("qL", "qM", "qG", "qH", "qC", "qD", "qE", "qF", "sV", "sS", "sT", "sG", "sCC", *["b_" + e for e in ENGS])

    def sync_fence():
        kb.wait("sync", "sT")
        kb.wait("sync", "sV")
        kb.wait("sync", "qG")
        kb.wait("sync", "qH")

    din = lambda name, shape, dtype: nc.dram_tensor(
        name, shape, dtype, kind="ExternalInput").ap()
    x0_d = din("x0T_own", [E, TPC], F32)
    wqkv_d = din("wqkvT", [16, 128, 6 * 128], F32)
    wp_d = din("wpT_own", [2, 128, 16 * 128], F32)
    pkg_d = din("pk_g", [4, 128, MLP], U16)
    pku_d = din("pk_u", [4, 128, MLP], U16)
    pkd_d = din("pk_d", [32, 128, E], U16)
    scg_d = din("scT_g", [64, MLP], F32)
    scu_d = din("scT_u", [64, MLP], F32)
    scd_d = din("scT_d", [512, E], F32)
    cs1_d = din("CS1", [128, T], F32)
    cs2_d = din("CS2", [128, T], F32)
    mask_d = din("maskM", [128, 896], BF16)
    ident_d = din("ident", [128, 128], BF16)

    out_d = nc.dram_tensor("out", [E, TPC], F32, kind="ExternalOutput").ap()

    ikind = "ExternalOutput" if DEBUG else "Internal"
    cc_b0_in = [nc.dram_tensor(f"cc_b0_in{h}", [E, 512], BF16,
                               kind="Internal").ap() for h in range(2)]
    cc_b0_out = [nc.dram_tensor(f"cc_b0_out{h}", [NC * E, 512], BF16,
                                addr_space="Shared", kind="Internal").ap()
                 for h in range(2)]
    rs_in = [nc.dram_tensor(f"rs_in{g}", [NC * E, PC], F32, kind="Internal").ap()
             for g in range(NG)]
    rs_out = [nc.dram_tensor(f"rs_out{g}", [E, PC], F32, kind="Internal").ap()
              for g in range(NG)]
    x0b_sp = nc.dram_tensor("x0b_sp", [E, TPC], BF16, kind=ikind).ap()
    acc_dram = nc.dram_tensor("acc_dram", [E, TPC], F32, kind="Internal").ap()
    x1b_sp = nc.dram_tensor("x1b_sp", [E, TPC], BF16, kind=ikind).ap()

    ones128 = es.enter_context(nc.sbuf_tensor("ones128", [128, 1], BF16))
    ones1 = es.enter_context(nc.sbuf_tensor("ones1", [1, 128], BF16))
    ones1f = es.enter_context(nc.sbuf_tensor("ones1f", [1, 128], F32))
    rw_m = es.enter_context(nc.sbuf_tensor("rw_m", [1, 512], F32))
    rw_v = es.enter_context(nc.sbuf_tensor("rw_v", [1, 512], F32))
    rw_t = es.enter_context(nc.sbuf_tensor("rw_t", [1, 512], F32))
    rw_mb = es.enter_context(nc.sbuf_tensor("rw_mb", [1, 512], BF16))
    rw_rb = es.enter_context(nc.sbuf_tensor("rw_rb", [1, 512], BF16))
    epsb = es.enter_context(nc.sbuf_tensor("epsb", [1, 1], F32))

    kb.step("vector", lambda v: v.memset(ones128[:], 1.0))
    kb.step("vector", lambda v: v.memset(ones1[:], 1.0))
    kb.step("vector", lambda v: v.memset(ones1f[:], 1.0))
    kb.step("vector", lambda v: v.memset(epsb[:], EPS), sem="sV")

    def ln_stats(strips, tw, ps_s, ps_s2, ps_bc, sqst, mB, rB, f32out=False):
        nstr = len(strips)
        kb.wait("tensor", "sV")
        kb.wait("scalar", "sV")
        s2_tgt = {}
        for j, xs in enumerate(strips):
            kb.step("tensor", lambda t, xs=xs, j=j: t.matmul(
                out=ps_s[0:1, 0:tw], lhsT=ones128[:], rhs=xs,
                start=(j == 0), stop=(j == nstr - 1)), sem="sT")
            slot = (j % 4) * tw
            if j >= 4:
                kb.wait("scalar", "sT", s2_tgt[j - 4])
            kb.step("scalar", lambda s, xs=xs, slot=slot: s.activation(
                out=sqst[:, slot:slot + tw], in_=xs, func=AF.Square), sem="sS")
            kb.wait("tensor", "sS")
            s2_tgt[j] = kb.step("tensor", lambda t, slot=slot, j=j: t.matmul(
                out=ps_s2[0:1, 0:tw], lhsT=ones128[:], rhs=sqst[:, slot:slot + tw],
                start=(j == 0), stop=(j == nstr - 1)), sem="sT")
        kb.wait("vector", "sT")
        kb.step("vector", lambda v: v.tensor_scalar_mul(
            rw_m[0:1, 0:tw], ps_s[0:1, 0:tw], R_E))
        kb.step("vector", lambda v: v.tensor_tensor(
            out=rw_t[0:1, 0:tw], in0=rw_m[0:1, 0:tw], in1=rw_m[0:1, 0:tw],
            op=ALU.mult))
        kb.step("vector", lambda v: v.scalar_tensor_tensor(
            out=rw_v[0:1, 0:tw], in0=ps_s2[0:1, 0:tw], scalar=R_E,
            in1=rw_t[0:1, 0:tw], op0=ALU.mult, op1=ALU.subtract), sem="sV")
        kb.wait("scalar", "sV")
        kb.step("scalar", lambda s: s.activation(
            out=rw_v[0:1, 0:tw], in_=rw_v[0:1, 0:tw], func=AF.Sqrt, bias=epsb[0:1, 0:1]),
            sem="sS")
        kb.wait("vector", "sS")
        kb.step("vector", lambda v: v.reciprocal(
            out=rw_v[0:1, 0:tw], in_=rw_v[0:1, 0:tw]))
        if f32out:
            kb.step("vector", lambda v: v.engine_nop(), sem="sV")
            m_rhs, r_rhs, l_ones = rw_m, rw_v, ones1f
        else:
            kb.step("vector", lambda v: v.tensor_copy(
                out=rw_mb[0:1, 0:tw], in_=rw_m[0:1, 0:tw]))
            kb.step("vector", lambda v: v.tensor_copy(
                out=rw_rb[0:1, 0:tw], in_=rw_v[0:1, 0:tw]), sem="sV")
            m_rhs, r_rhs, l_ones = rw_mb, rw_rb, ones1
        kb.wait("tensor", "sV")
        kb.step("tensor", lambda t: t.matmul(
            out=ps_bc[:, 0:tw], lhsT=l_ones[:], rhs=m_rhs[0:1, 0:tw],
            start=True, stop=True), sem="sT")
        kb.wait("vector", "sT")
        kb.step("vector", lambda v: v.tensor_copy(
            out=mB[:, 0:tw], in_=ps_bc[:, 0:tw]), sem="sV")
        kb.wait("tensor", "sV")
        kb.step("tensor", lambda t: t.matmul(
            out=ps_bc[:, 0:tw], lhsT=l_ones[:], rhs=r_rhs[0:1, 0:tw],
            start=True, stop=True), sem="sT")
        kb.wait("vector", "sT")
        kb.step("vector", lambda v: v.tensor_copy(
            out=rB[:, 0:tw], in_=ps_bc[:, 0:tw]), sem="sV")

    es_w = ExitStack()
    wqkvb = es_w.enter_context(nc.sbuf_tensor("wqkvb", [128, 6 * 16 * 128], BF16))
    wpb = es_w.enter_context(nc.sbuf_tensor("wpb", [128, 2 * 16 * 128], BF16))
    maskM = es_w.enter_context(nc.sbuf_tensor("maskM_s", [128, 896], BF16))
    identb = es_w.enter_context(nc.sbuf_tensor("identb", [128, 128], BF16))

    kb.dma("sync", identb[:], ident_d, "qC")
    kb.dma("sync", maskM[:], mask_d, "qC")
    kb.wait("tensor", "qC")
    kb.wait("vector", "qC")

    es0 = ExitStack()
    p0f = es0.enter_context(nc.sbuf_tensor("p0f", [128, 2 * 2048], F32))
    x0bs = es0.enter_context(nc.sbuf_tensor("x0bs", [128, 16 * TPC], BF16))
    sqst0 = es0.enter_context(nc.sbuf_tensor("sqst0", [128, 2048], BF16))
    nrm0 = es0.enter_context(nc.sbuf_tensor("nrm0", [128, 2 * 512], BF16))
    mB0 = es0.enter_context(nc.sbuf_tensor("mB0", [128, 512], BF16))
    rB0 = es0.enter_context(nc.sbuf_tensor("rB0", [128, 512], BF16))
    ps_s = es0.enter_context(nc.psum_tensor("ps_s", [1, 512], F32))
    ps_s2 = es0.enter_context(nc.psum_tensor("ps_s2", [1, 512], F32))
    ps_bc = es0.enter_context(nc.psum_tensor("ps_bc", [128, 512], F32))

    tile_jobs = []
    for ec in range(16):
        dst = wqkvb[:].rearrange("p (f e c) -> p f e c", f=6, e=16)[:, :, ec, :]
        tile_jobs.append((wqkv_d[ec], dst, 768))
    for e2 in range(2):
        dst = wpb[:, e2 * 2048:(e2 + 1) * 2048].rearrange(
            "p (f c) -> p f c", f=16)
        tile_jobs.append((wp_d[e2], dst, 2048))
    for j, (srcap, dst, w) in enumerate(tile_jobs):
        par = j % 2
        ldsem = "qL" if par == 0 else "qM"
        if j >= 2:
            kb.wait("sync", "sV")
        kb.dma("sync", p0f[:, 2048 * par:2048 * par + w], srcap, ldsem)
        kb.wait("vector", ldsem)
        kb.step("vector", lambda v, par=par, dst=dst, w=w: v.tensor_copy(
            out=dst, in_=p0f[:, 2048 * par:2048 * par + w].rearrange(
                "p (a c) -> p a c", c=128)), sem="sV")


    for ec in range(16):
        par = ec % 2
        ldsem = "qL" if par == 0 else "qM"
        kb.wait("sync", "sV")
        kb.dma("sync", p0f[:, 2048 * par:2048 * par + 1024],
               x0_d[128 * ec:128 * (ec + 1), :], ldsem)
        kb.wait("vector", ldsem)
        kb.step("vector", lambda v, ec=ec, par=par: v.tensor_copy(
            out=x0bs[:, TPC * ec:TPC * (ec + 1)],
            in_=p0f[:, 2048 * par:2048 * par + 1024]), sem="sV")
    kb.wait("gpsimd", "sV")
    for ec in range(16):
        kb.dma("gpsimd", x0b_sp[128 * ec:128 * (ec + 1), :],
               x0bs[:, TPC * ec:TPC * (ec + 1)], "qG")

    ag_tgt = []
    for tb in range(2):
        strips = [x0bs[:, TPC * ec + 512 * tb: TPC * ec + 512 * (tb + 1)]
                  for ec in range(16)]
        ln_stats(strips, 512, ps_s, ps_s2, ps_bc, sqst0, mB0, rB0)
        for ec in range(16):
            par = ec % 2
            src = strips[ec]
            stsem = "qG" if par == 0 else "qH"
            kb.wait("vector", stsem)
            kb.step("vector", lambda v, src=src, par=par: v.tensor_tensor(
                out=nrm0[:, 512 * par:512 * (par + 1)], in0=src, in1=mB0[:],
                op=ALU.subtract))
            kb.step("vector", lambda v, par=par: v.tensor_tensor(
                out=nrm0[:, 512 * par:512 * (par + 1)],
                in0=nrm0[:, 512 * par:512 * (par + 1)], in1=rB0[:],
                op=ALU.mult), sem="sV")
            kb.wait("gpsimd", "sV")
            kb.dma("gpsimd",
                   cc_b0_in[tb][128 * ec:128 * (ec + 1), :],
                   nrm0[:, 512 * par:512 * (par + 1)], stsem)
        kb.wait("gpsimd", "qG")
        kb.wait("gpsimd", "qH")
        kb.step("gpsimd", lambda g, tb=tb: g.collective_compute(
            "AllGather", ALU.bypass, ins=[cc_b0_in[tb]], outs=[cc_b0_out[tb]],
            replica_groups=[list(range(NC))]), sem="sCC")
        ag_tgt.append(kb.cnt["sCC"])

    sync_fence()
    es0.close()

    es12 = ExitStack()
    Kt = es12.enter_context(nc.sbuf_tensor("Kt", [128, 2 * T], BF16))
    Vt = es12.enter_context(nc.sbuf_tensor("Vt", [128, 2 * T], BF16))
    Qt = es12.enter_context(nc.sbuf_tensor("Qt", [128, 2 * T], BF16))

    es1 = ExitStack()
    b0c = es1.enter_context(nc.sbuf_tensor("b0c", [128, 2 * 16 * 512], BF16))
    zb = es1.enter_context(nc.sbuf_tensor("zb", [128, 6 * 512], BF16))
    rt1 = es1.enter_context(nc.sbuf_tensor("rt1", [128, 512], BF16))
    rt2 = es1.enter_context(nc.sbuf_tensor("rt2", [128, 512], BF16))
    csb = es1.enter_context(nc.sbuf_tensor("csb", [128, 2 * 1024], BF16))
    csf = es1.enter_context(nc.sbuf_tensor("csf", [128, 2 * 1024], F32))
    ps_qkv = [es1.enter_context(nc.psum_tensor(f"ps_qkv{f}", [128, 512], F32))
              for f in range(6)]
    ps_vt = es1.enter_context(nc.psum_tensor("ps_vt", [128, 512], BF16))


    evac_done = 0
    evac_f = {}
    b0_loaded = {}
    mm_done = {}
    cs_cast = {}
    for tc in range(16 + 1):
        par = tc % 2
        if tc < 16:
            ldsem = "qL" if par == 0 else "qM"
            if tc >= 2:
                kb.wait("sync", "sT", mm_done[tc - 2])
            r0 = 2 * (tc % 4)
            half = tc // 8
            lc = 256 * (tc // 4) - 512 * half
            kb.wait("sync", "sCC", ag_tgt[half])
            for ec in range(16):
                for rr in range(2):
                    kb.dma("sync",
                           b0c[:, (par * 16 + ec) * 512 + 256 * rr:
                               (par * 16 + ec) * 512 + 256 * (rr + 1)],
                           cc_b0_out[half][E * (r0 + rr) + 128 * ec:
                                           E * (r0 + rr) + 128 * (ec + 1),
                                           lc:lc + 256],
                           ldsem)
            if tc >= 2:
                kb.wait("sync", "sV", cs_cast.get(tc - 2, 0))
            kb.dma("sync", csf[:, 1024 * par:1024 * par + 512],
                   cs1_d[:, 512 * tc:512 * (tc + 1)], ldsem)
            kb.dma("sync", csf[:, 1024 * par + 512:1024 * (par + 1)],
                   cs2_d[:, 512 * tc:512 * (tc + 1)], ldsem)
            b0_loaded[tc] = (ldsem, kb.cnt[ldsem])
            kb.wait("vector", *b0_loaded[tc])
            kb.step("vector", lambda v, par=par: v.tensor_copy(
                out=csb[:, 1024 * par:1024 * (par + 1)],
                in_=csf[:, 1024 * par:1024 * (par + 1)]), sem="sV")
            cs_cast[tc] = kb.cnt["sV"]
        if tc >= 1:
            tcc = tc - 1
            parc = tcc % 2
            kb.wait("tensor", *b0_loaded[tcc])
            for f in range(6):
                if tcc >= 1:
                    kb.wait("tensor", "sV", evac_f[(tcc - 1, f)])
                for ec in range(16):
                    w_ap = wqkvb[:, (f * 16 + ec) * 128:(f * 16 + ec + 1) * 128]
                    b_ap = b0c[:, (parc * 16 + ec) * 512:(parc * 16 + ec + 1) * 512]
                    kb.step("tensor", lambda t, w=w_ap, b=b_ap, f=f, ec=ec: t.matmul(
                        out=ps_qkv[f][:], lhsT=w, rhs=b,
                        start=(ec == 0), stop=(ec == 15)),
                        sem="sT" if ec == 15 else None)
            mm_done[tcc] = kb.cnt["sT"]
            cs1sl = csb[:, 1024 * parc:1024 * parc + 512]
            cs2sl = csb[:, 1024 * parc + 512:1024 * (parc + 1)]
            mmb = mm_done[tcc] - 6
            for f in range(4):
                hh = f % 2
                dst = Qt if f < 2 else Kt
                dsl = slice(T * hh + 512 * tcc, T * hh + 512 * (tcc + 1))
                zsl = zb[:, 512 * f:512 * (f + 1)]
                kb.wait("vector", "sT", mmb + f + 1)
                evac_f[(tcc, f)] = kb.step(
                    "vector", lambda v, f=f, zsl=zsl: v.tensor_copy(
                        out=zsl, in_=ps_qkv[f][:]), sem="sV")
                kb.step("vector", lambda v, zsl=zsl, cs1sl=cs1sl: v.tensor_tensor(
                    out=rt1[:], in0=zsl, in1=cs1sl, op=ALU.mult))
                kb.step("vector", lambda v, zsl=zsl, cs2sl=cs2sl: v.tensor_tensor(
                    out=rt2[0:64, :], in0=zsl[64:128, :], in1=cs2sl[64:128, :],
                    op=ALU.mult))
                kb.step("vector", lambda v, zsl=zsl, cs2sl=cs2sl: v.tensor_tensor(
                    out=rt2[64:128, :], in0=zsl[0:64, :], in1=cs2sl[0:64, :],
                    op=ALU.mult))
                kb.step("vector", lambda v, dst=dst, dsl=dsl: v.tensor_tensor(
                    out=dst[:, dsl], in0=rt1[:], in1=rt2[:], op=ALU.subtract),
                    sem="sV")
            for f in (4, 5):
                hh = f % 2
                dsl = slice(T * hh + 512 * tcc, T * hh + 512 * (tcc + 1))
                zsl = zb[:, 512 * f:512 * (f + 1)]
                kb.wait("vector", "sT", mmb + f + 1)
                evac_f[(tcc, f)] = kb.step(
                    "vector", lambda v, f=f, zsl=zsl: v.tensor_copy(
                        out=zsl, in_=ps_qkv[f][:]), sem="sV")
                kb.wait("tensor", "sV")
                for j in range(4):
                    kb.step("tensor", lambda t, zsl=zsl, j=j: t.transpose(
                        out=ps_vt[:, 128 * j:128 * (j + 1)],
                        in_=zsl[:, 128 * j:128 * (j + 1)], identity=identb[:]),
                        sem="sT" if j == 3 else None)
                kb.wait("vector", "sT")
                kb.step("vector", lambda v, dsl=dsl: v.tensor_copy(
                    out=Vt[:, dsl], in_=ps_vt[:]), sem="sV")
            evac_done = kb.cnt["sV"]

    kb.barrier()
    es1.close()

    if DEBUG:
        kqv_dbg = {nm: nc.dram_tensor(nm + "_dbg", [128, 2 * T], BF16,
                                      kind="ExternalOutput").ap()
                   for nm in ("Kt", "Qt", "Vt")}
        for nm, tile in (("Kt", Kt), ("Qt", Qt), ("Vt", Vt)):
            kb.dma("gpsimd", kqv_dbg[nm], tile[:], "qG")
        b0_dbg = nc.dram_tensor("b0_dbg", [2 * NC * E, 512], BF16,
                                kind="ExternalOutput").ap()
        for h in range(2):
            kb.dma("gpsimd", b0_dbg[NC * E * h:NC * E * (h + 1), :],
                   cc_b0_out[h], "qG")
        wq_dbg = nc.dram_tensor("wq_dbg", [128, 6 * 16 * 128], BF16,
                                kind="ExternalOutput").ap()
        kb.dma("gpsimd", wq_dbg, wqkvb[:], "qG")
        kb.wait("gpsimd", "qG")
        kb.barrier()

    es2p = ExitStack()
    Pt = es2p.enter_context(nc.sbuf_tensor("Pt", [128, 2 * 1024], BF16))
    Ygrp = es2p.enter_context(nc.sbuf_tensor("Ygrp", [128, 2 * GT], BF16))
    rlB = es2p.enter_context(nc.sbuf_tensor("rlB", [128, 1024], BF16))
    oev = es2p.enter_context(nc.sbuf_tensor("oev", [128, 2 * 512], F32))

    exp_done = {}
    prev_proj_sv = 0
    pend_tail = None
    st_yh = {"t": 0}
    for g in range(NG):
        esg = ExitStack()
        ps_sc = [esg.enter_context(nc.psum_tensor(f"ps_sc{g}_{j}", [128, 1024], F32))
                 for j in range(2)]
        ps_y = [esg.enter_context(nc.psum_tensor(f"ps_y{g}_{j}", [128, 512], F32))
                for j in range(2)]
        ps_l = [esg.enter_context(nc.psum_tensor(f"ps_l{g}_{j}", [128, 512], F32))
                for j in range(2)]
        kb.wait("tensor", "sV", prev_proj_sv)
        for hh in range(2):
            for qcp_in_g in range(2):
                qcp = 2 * g + qcp_in_g
                qc0, qc1 = 2 * qcp, 2 * qcp + 1
                kmax0, kmax1 = 4 * qc0 + 3, 4 * qc1 + 3
                need_v = {}
                for kt in range(kmax1 + 2):
                    if kt <= kmax1:
                        par = kt % 2
                        both = kt <= kmax0
                        prev = exp_done.get((hh, qcp, kt - 2))
                        if prev is None:
                            prev = kb.cnt["sS"] if kt < 2 else None
                        if prev:
                            kb.wait("tensor", "sS", prev)
                        q_ap1 = Qt[:, T * hh + 512 * qc1: T * hh + 512 * (qc1 + 1)]
                        k_ap = Kt[:, T * hh + 128 * kt: T * hh + 128 * (kt + 1)]
                        if both:
                            q_ap0 = Qt[:, T * hh + 512 * qc0: T * hh + 512 * (qc0 + 1)]
                            kb.step("tensor", lambda t, k=k_ap, q=q_ap0, par=par:
                                    t.matmul(out=ps_sc[par][:, 0:512], lhsT=k, rhs=q,
                                             start=True, stop=True))
                        kb.step("tensor", lambda t, k=k_ap, q=q_ap1, par=par:
                                t.matmul(out=ps_sc[par][:, 512:1024], lhsT=k, rhs=q,
                                         start=True, stop=True), sem="sT")
                        kb.wait("scalar", "sT")
                        if both:
                            kb.step("scalar", lambda s, par=par: s.activation(
                                out=Pt[:, 1024 * par:1024 * (par + 1)],
                                in_=ps_sc[par][:], func=AF.Exp, scale=C_SCALE),
                                sem="sS")
                        else:
                            kb.step("scalar", lambda s, par=par: s.activation(
                                out=Pt[:, 1024 * par + 512:1024 * (par + 1)],
                                in_=ps_sc[par][:, 512:1024], func=AF.Exp,
                                scale=C_SCALE), sem="sS")
                        exp_done[(hh, qcp, kt)] = kb.cnt["sS"]
                        nv = 0
                        if both and 4 * qc0 <= kt <= 4 * qc0 + 3:
                            off = 384 - 128 * (kt - 4 * qc0)
                            kb.wait("vector", "sS")
                            kb.step("vector", lambda v, par=par, off=off:
                                    v.tensor_tensor(
                                        out=Pt[:, 1024 * par:1024 * par + 512],
                                        in0=Pt[:, 1024 * par:1024 * par + 512],
                                        in1=maskM[:, off:off + 512], op=ALU.mult),
                                    sem="sV")
                            nv = kb.cnt["sV"]
                        if 4 * qc1 <= kt <= 4 * qc1 + 3:
                            off = 384 - 128 * (kt - 4 * qc1)
                            kb.wait("vector", "sS")
                            kb.step("vector", lambda v, par=par, off=off:
                                    v.tensor_tensor(
                                        out=Pt[:, 1024 * par + 512:1024 * (par + 1)],
                                        in0=Pt[:, 1024 * par + 512:1024 * (par + 1)],
                                        in1=maskM[:, off:off + 512], op=ALU.mult),
                                    sem="sV")
                            nv = kb.cnt["sV"]
                        need_v[kt] = nv
                    if kt == 1 and pend_tail is not None:
                        pend_tail()
                        pend_tail = None
                    if kt >= 1:
                        ky = kt - 1
                        pary = ky % 2
                        bothy = ky <= kmax0
                        if ky == 0 and st_yh["t"]:
                            kb.wait("tensor", "sV", st_yh["t"])
                        kb.wait("tensor", "sS", exp_done[(hh, qcp, ky)])
                        if need_v[ky]:
                            kb.wait("tensor", "sV", need_v[ky])
                        v_ap = Vt[:, T * hh + 128 * ky: T * hh + 128 * (ky + 1)]
                        p0_ap = Pt[:, 1024 * pary:1024 * pary + 512]
                        p1_ap = Pt[:, 1024 * pary + 512:1024 * (pary + 1)]
                        if bothy:
                            kb.step("tensor", lambda t, v=v_ap, p=p0_ap, ky=ky,
                                    k0=kmax0: t.matmul(out=ps_y[0][:], lhsT=v,
                                                       rhs=p, start=(ky == 0),
                                                       stop=(ky == k0)))
                            kb.step("tensor", lambda t, p=p0_ap, ky=ky, k0=kmax0:
                                    t.matmul(out=ps_l[0][0:1, :], lhsT=ones128[:],
                                             rhs=p, start=(ky == 0),
                                             stop=(ky == k0)))
                        kb.step("tensor", lambda t, v=v_ap, p=p1_ap, ky=ky,
                                k1=kmax1: t.matmul(out=ps_y[1][:], lhsT=v, rhs=p,
                                                   start=(ky == 0),
                                                   stop=(ky == k1)))
                        kb.step("tensor", lambda t, p=p1_ap, ky=ky, k1=kmax1:
                                t.matmul(out=ps_l[1][0:1, :], lhsT=ones128[:],
                                         rhs=p, start=(ky == 0), stop=(ky == k1)),
                                sem="sT" if ky == kmax1 else None)
                kb.wait("vector", "sT")
                kb.step("vector", lambda v: v.reciprocal(
                    out=rw_m[0:1, 0:512], in_=ps_l[0][0:1, :]))
                kb.step("vector", lambda v: v.reciprocal(
                    out=rw_v[0:1, 0:512], in_=ps_l[1][0:1, :]))
                kb.step("vector", lambda v: v.tensor_copy(
                    out=rw_mb[0:1, 0:512], in_=rw_m[0:1, 0:512]))
                kb.step("vector", lambda v: v.tensor_copy(
                    out=rw_rb[0:1, 0:512], in_=rw_v[0:1, 0:512]), sem="sV")
                vA = kb.cnt["sV"]

                def mk_tail(hh=hh, qc0=qc0, qc1=qc1, vA=vA, g=g,
                            ps_l=ps_l, ps_y=ps_y, Ygrp=Ygrp):
                    def tail():
                        kb.wait("tensor", "sV", vA)
                        kb.step("tensor", lambda t: t.matmul(
                            out=ps_l[0][:], lhsT=ones1[:],
                            rhs=rw_mb[0:1, 0:512], start=True, stop=True))
                        kb.step("tensor", lambda t: t.matmul(
                            out=ps_l[1][:], lhsT=ones1[:],
                            rhs=rw_rb[0:1, 0:512], start=True, stop=True),
                            sem="sT")
                        bc = kb.cnt["sT"]
                        kb.wait("vector", "sT", bc)
                        for j in range(2):
                            kb.step("vector", lambda v, j=j: v.tensor_copy(
                                out=rlB[:, 512 * j:512 * (j + 1)],
                                in_=ps_l[j][:]))
                        for j, qc in ((0, qc0), (1, qc1)):
                            ysl = Ygrp[:, GT * hh + 512 * (qc - 4 * g):
                                       GT * hh + 512 * (qc - 4 * g + 1)]
                            kb.step("vector", lambda v, j=j, ysl=ysl:
                                    v.tensor_tensor(out=ysl, in0=ps_y[j][:],
                                                    in1=rlB[:, 512 * j:
                                                            512 * (j + 1)],
                                                    op=ALU.mult), sem="sV")
                        st_yh["t"] = kb.cnt["sV"]
                    return tail
                pend_tail = mk_tail()
        pend_tail()
        pend_tail = None
        esg.close()
        esp = ExitStack()
        ps_o = [esp.enter_context(nc.psum_tensor(f"ps_o{g}_{j}", [128, 512], F32))
                for j in range(4)]
        proj_sv_base = kb.cnt["sV"]
        kb.wait("tensor", "sV")
        kb.wait("vector", "qG")
        kb.wait("vector", "qH")
        nproj = 0
        for tch in range(4):
            for fc in range(16):
                j = nproj % 4
                if nproj >= 4:
                    kb.wait("tensor", "sV", proj_sv_base + nproj - 3)
                for e2 in range(2):
                    w_ap = wpb[:, (e2 * 16 + fc) * 128:(e2 * 16 + fc + 1) * 128]
                    y_ap = Ygrp[:, GT * e2 + 512 * tch: GT * e2 + 512 * (tch + 1)]
                    kb.step("tensor", lambda t, w=w_ap, y=y_ap, j=j, e2=e2:
                            t.matmul(out=ps_o[j][:], lhsT=w, rhs=y,
                                     start=(e2 == 0), stop=(e2 == 1)),
                            sem="sT" if e2 == 1 else None)
                kb.wait("vector", "sT")
                jo = nproj % 2
                stsem = "qG" if jo == 0 else "qH"
                if nproj >= 2:
                    kb.wait("vector", stsem)
                kb.step("vector", lambda v, j=j, jo=jo: v.tensor_copy(
                    out=oev[:, 512 * jo:512 * (jo + 1)], in_=ps_o[j][:]), sem="sV")
                kb.wait("gpsimd", "sV")
                for rr in range(2):
                    p = 2 * tch + rr
                    kb.dma("gpsimd",
                           rs_in[g][E * p + 128 * fc: E * p + 128 * (fc + 1), :],
                           oev[:, 512 * jo + 256 * rr: 512 * jo + 256 * (rr + 1)],
                           stsem)
                nproj += 1
        prev_proj_sv = kb.cnt["sV"]
        kb.wait("gpsimd", "qG")
        kb.wait("gpsimd", "qH")
        if DEBUG:
            rsin_dbg = nc.dram_tensor(f"rsin_dbg{g}", [NC * E, PC], F32,
                                      kind="ExternalOutput").ap()
            kb.dma("gpsimd", rsin_dbg, rs_in[g], "qG")
            kb.wait("gpsimd", "qG")
        kb.step("gpsimd", lambda gg, g=g: gg.collective_compute(
            "ReduceScatter", ALU.add, ins=[rs_in[g]], outs=[rs_out[g]],
            replica_groups=[list(range(NC))]), sem="sCC")
        esp.close()

    sync_fence()
    es2p.close()
    es12.close()
    es_w.close()

    es34 = ExitStack()
    b_ffn = es34.enter_context(nc.sbuf_tensor("b_ffn", [128, 16 * TPC], BF16))

    es3 = ExitStack()
    x0r = es3.enter_context(nc.sbuf_tensor("x0r", [128, 16 * TPC], BF16))
    x1bs = es3.enter_context(nc.sbuf_tensor("x1bs", [128, 16 * TPC], BF16))
    oBs = es3.enter_context(nc.sbuf_tensor("oBs", [128, 16 * 256], BF16))
    oF = es3.enter_context(nc.sbuf_tensor("oF", [128, 2 * 4096], F32))
    sqst3 = es3.enter_context(nc.sbuf_tensor("sqst3", [128, 2048], BF16))
    mB3 = es3.enter_context(nc.sbuf_tensor("mB3", [128, 512], BF16))
    rB3 = es3.enter_context(nc.sbuf_tensor("rB3", [128, 512], BF16))
    ps_s3 = es3.enter_context(nc.psum_tensor("ps_s3", [1, 512], F32))
    ps_s23 = es3.enter_context(nc.psum_tensor("ps_s23", [1, 512], F32))
    ps_bc3 = es3.enter_context(nc.psum_tensor("ps_bc3", [128, 512], F32))

    for ec in range(16):
        kb.dma("sync", x0r[:, TPC * ec:TPC * (ec + 1)],
               x0b_sp[128 * ec:128 * (ec + 1), :], "qC")
    sCC_rs0 = kb.cnt["sCC"] - NG
    if DEBUG:
        rs_dbg = nc.dram_tensor("rs_dbg", [E, PC], F32, kind="ExternalOutput").ap()
        kb.wait("gpsimd", "sCC", sCC_rs0 + 1)
        kb.dma("gpsimd", rs_dbg, rs_out[0], "qG")
    cast_done = {}
    for g in range(NG):
        slot = g % 2
        ldsem3 = "qD" if slot == 0 else "qE"
        kb.wait("sync", "sCC", sCC_rs0 + 1 + g)
        if g >= 2:
            kb.wait("sync", "sV", cast_done[g - 2])
        for ec in range(16):
            kb.dma("sync", oF[:, 4096 * slot + 256 * ec:
                              4096 * slot + 256 * (ec + 1)],
                   rs_out[g][128 * ec:128 * (ec + 1), :], ldsem3)
        kb.wait("vector", ldsem3)
        for ec in range(16):
            kb.step("vector", lambda v, ec=ec, slot=slot: v.tensor_copy(
                out=oBs[:, 256 * ec:256 * (ec + 1)],
                in_=oF[:, 4096 * slot + 256 * ec:4096 * slot + 256 * (ec + 1)]),
                sem="sV")
        cast_done[g] = kb.cnt["sV"]
        strips = [oBs[:, 256 * ec:256 * (ec + 1)] for ec in range(16)]
        ln_stats(strips, 256, ps_s3, ps_s23, ps_bc3, sqst3, mB3, rB3)
        kb.wait("vector", "qC")
        for ec in range(16):
            osl = oBs[:, 256 * ec:256 * (ec + 1)]
            x0sl = x0r[:, TPC * ec + 256 * g: TPC * ec + 256 * (g + 1)]
            x1sl = x1bs[:, TPC * ec + 256 * g: TPC * ec + 256 * (g + 1)]
            kb.step("vector", lambda v, osl=osl: v.tensor_tensor(
                out=osl, in0=osl, in1=mB3[:, 0:256], op=ALU.subtract))
            kb.step("vector", lambda v, osl=osl: v.tensor_tensor(
                out=osl, in0=osl, in1=rB3[:, 0:256], op=ALU.mult))
            kb.step("vector", lambda v, osl=osl, x0sl=x0sl, x1sl=x1sl:
                    v.tensor_tensor(out=x1sl, in0=osl, in1=x0sl, op=ALU.add),
                    sem="sV")

    kb.wait("gpsimd", "sV")
    for ec in range(16):
        kb.dma("gpsimd", x1b_sp[128 * ec:128 * (ec + 1), :],
               x1bs[:, TPC * ec:TPC * (ec + 1)], "qG")
    for tb in range(2):
        strips = [x1bs[:, TPC * ec + 512 * tb: TPC * ec + 512 * (tb + 1)]
                  for ec in range(16)]
        ln_stats(strips, 512, ps_s3, ps_s23, ps_bc3, sqst3, mB3, rB3)
        for ec in range(16):
            src = strips[ec]
            dst = b_ffn[:, TPC * ec + 512 * tb: TPC * ec + 512 * (tb + 1)]
            kb.step("vector", lambda v, src=src, dst=dst: v.tensor_tensor(
                out=dst, in0=src, in1=mB3[:], op=ALU.subtract))
            kb.step("vector", lambda v, dst=dst: v.tensor_tensor(
                out=dst, in0=dst, in1=rB3[:], op=ALU.mult), sem="sV")

    sync_fence()
    es3.close()

    if DEBUG:
        bffn_dbg = nc.dram_tensor("bffn_dbg", [E, TPC], BF16,
                                  kind="ExternalOutput").ap()
        kb.wait("gpsimd", "sV")
        for ec in range(16):
            kb.dma("gpsimd", bffn_dbg[128 * ec:128 * (ec + 1), :],
                   b_ffn[:, TPC * ec:TPC * (ec + 1)], "qG")
        kb.wait("gpsimd", "qG")
        kb.barrier()

    FW = 512
    FWINS = MLP // FW
    es4 = ExitStack()
    gwin = es4.enter_context(nc.sbuf_tensor("gwin", [128, 8 * TPC], BF16))
    es4a = ExitStack()
    w4buf = [es4a.enter_context(nc.sbuf_tensor(f"w4buf{s}", [128, 16384], U16))
             for s in range(2)]
    fstage = es4a.enter_context(nc.sbuf_tensor("fstage", [128, 2 * 512], F32))
    pkbuf = es4a.enter_context(nc.sbuf_tensor("pkbuf", [128, 2 * 4096], U16))
    s16buf = es4a.enter_context(nc.sbuf_tensor("s16buf", [128, 2 * 4096], BF16))
    scf = es4a.enter_context(nc.sbuf_tensor("scf", [128, 4096], F32))
    dq1 = es4a.enter_context(nc.sbuf_tensor("dq1", [128, 2048], U16))
    dq2 = es4a.enter_context(nc.sbuf_tensor("dq2", [128, 2048], U16))
    ps_gu = [es4a.enter_context(nc.psum_tensor(f"ps_gu{j}", [128, 512], F32))
             for j in range(4)]
    ps_dn = [es4a.enter_context(nc.psum_tensor(f"ps_dn{j}", [128, 512], F32))
             for j in range(4)]


    phases = []
    for w in range(FWINS):
        phases.append(("g", w))
        phases.append(("u", w))
        if w % 2 == 1:
            phases.append(("d", w // 2))

    def emit_phase_loads(kind, w, slot):
        ldsem = "qL" if slot == 0 else "qM"
        scsem = "qC" if slot == 0 else "qF"
        if kind in ("g", "u"):
            pk_ = pkg_d if kind == "g" else pku_d
            sc_ = scg_d if kind == "g" else scu_d
            for c in range(4):
                kb.dma("sync", pkbuf[:, 4096 * slot + 512 * c:
                                     4096 * slot + 512 * (c + 1)],
                       pk_[c, :, FW * w:FW * (w + 1)], ldsem)
                srcap = sc_[16 * c:16 * (c + 1), FW * w:FW * (w + 1)]
                srcap = srcap.unsqueeze(1).broadcast_to([16, 8, FW])
                kb.dma("sync", scf[:, 512 * c:512 * (c + 1)], srcap, scsem)
        else:
            for cc in range(2):
                c = 2 * w + cc
                kb.dma("sync", pkbuf[:, 4096 * slot + 2048 * cc:
                                     4096 * slot + 2048 * (cc + 1)],
                       pkd_d[c], ldsem)
                srcap = scd_d[16 * c:16 * (c + 1), :]
                srcap = srcap.unsqueeze(1).broadcast_to([16, 8, E])
                kb.dma("sync", scf[:, 2048 * cc:2048 * (cc + 1)], srcap, scsem)
        return (ldsem, kb.cnt[ldsem], scsem, kb.cnt[scsem])

    load_tgt = {0: emit_phase_loads(*phases[0], slot=0)}

    SH_S = (12, 8, 4, 0)
    SH_M = (9, 5, 1, -3)

    SH_S2 = (12, 8, 4, 0)

    def make_jobs(pi):
        kind, w = phases[pi]
        slot = pi % 2
        nch, width = (4, FW) if kind in ("g", "u") else (2, 2048)
        tiles = [(c, i) for c in range(nch) for i in range(4)]
        jobs = []
        for idx, (c, i) in enumerate(tiles):
            def job(idx=idx, c=c, i=i, slot=slot, width=width):
                tl = (4 * c + i) * width
                pk_ap = pkbuf[:, 4096 * slot + width * c:
                              4096 * slot + width * (c + 1)]
                s16_ap = s16buf[:, 4096 * slot + width * c:
                                4096 * slot + width * (c + 1)]
                d1 = dq1[:, 0:width]
                d2 = dq2[:, 0:width]
                sh_s, sh_m = SH_S[i], SH_M[i]
                kb.step("vector", lambda v, pk=pk_ap, sh=sh_s, d1=d1:
                        v.tensor_scalar(out=d1, in0=pk, scalar1=sh,
                                        scalar2=0x8000,
                                        op0=ALU.logical_shift_left,
                                        op1=ALU.bitwise_and))
                op = (ALU.logical_shift_left if sh_m >= 0
                      else ALU.logical_shift_right)
                kb.step("vector", lambda v, pk=pk_ap, sh=abs(sh_m), op=op, d2=d2:
                        v.tensor_scalar(out=d2, in0=pk, scalar1=sh,
                                        scalar2=0x0E00, op0=op,
                                        op1=ALU.bitwise_and))
                kb.step("vector", lambda v, tl=tl, d1=d1, d2=d2, slot=slot,
                        width=width: v.tensor_tensor(
                            out=w4buf[slot][:, tl:tl + width], in0=d1, in1=d2,
                            op=ALU.bitwise_or))
                kb.step("vector", lambda v, tl=tl, s16=s16_ap, slot=slot,
                        width=width: v.tensor_tensor(
                            out=w4buf[slot][:, tl:tl + width].bitcast(BF16),
                            in0=w4buf[slot][:, tl:tl + width].bitcast(FP16),
                            in1=s16, op=ALU.mult), sem="sV")
            jobs.append(job)
        return jobs

    from collections import deque
    mm_tgt = {}
    dq_done = {}
    queued = {}
    dq_fin = set()

    mult_tgt = {}

    def queue_phase(p):
        if p in queued or p in dq_fin or p >= len(phases):
            return
        ldsem, ldc, scsem, scc = load_tgt[p]
        slot = p % 2
        if p >= 2:
            kb.wait("scalar", "sV", dq_done[p - 2])
        kb.wait("scalar", scsem, scc)
        mult_tgt[p] = kb.step("scalar", lambda s, slot=slot: s.activation(
            out=s16buf[:, 4096 * slot:4096 * (slot + 1)], in_=scf[:, 0:4096],
            func=AF.Copy, scale=16384.0), sem="sS")
        kb.wait("vector", ldsem, ldc)
        kb.wait("vector", "sS", mult_tgt[p])
        if p >= 2:
            kb.wait("vector", "sT", mm_tgt[p - 2])
        queued[p] = deque(make_jobs(p))

    queue_phase(0)
    kb.wait("sync", "sS", mult_tgt[0])
    load_tgt[1] = emit_phase_loads(*phases[1], slot=1)

    def pop_jobs(p, n):
        q = queued.get(p)
        while q and n > 0:
            q.popleft()()
            n -= 1
        if q is not None and not q:
            dq_done[p] = kb.cnt["sV"]
            dq_fin.add(p)
            del queued[p]

    prev_gu = None
    prev_d = None
    for pi, (kind, w) in enumerate(phases):
        slot = pi % 2
        queue_phase(pi)
        pop_jobs(pi, 99)
        kb.wait("tensor", "sV", dq_done[pi])
        queue_phase(pi + 1)
        if pi + 2 < len(phases):
            kb.wait("sync", "sV", dq_done[pi])
            kb.wait("sync", "sS", mult_tgt[pi + 1])
            load_tgt[pi + 2] = emit_phase_loads(*phases[pi + 2], slot=(pi + 2) % 2)
        if kind in ("g", "u"):
            if kind == "u":
                kb.wait("vector", "sS")
            evsem = "sS" if kind == "g" else "sV"
            evac_tgt = {}
            gi = 0
            for nt in range(4):
                for tb in range(2):
                    j = gi % 4
                    if gi >= 4:
                        kb.wait("tensor", evsem, evac_tgt[gi - 4])
                    elif prev_gu is not None:
                        pv_evac, pv_sem, pv_n = prev_gu
                        kb.wait("tensor", pv_sem, pv_evac[pv_n - 4 + gi])
                    for ci in range(16):
                        w_ap = w4buf[slot][:, ci * FW + 128 * nt:
                                           ci * FW + 128 * (nt + 1)].bitcast(BF16)
                        b_ap = b_ffn[:, TPC * ci + 512 * tb:
                                     TPC * ci + 512 * (tb + 1)]
                        kb.step("tensor", lambda t, wv=w_ap, b=b_ap, ci=ci, j=j:
                                t.matmul(out=ps_gu[j][:], lhsT=wv, rhs=b,
                                         start=(ci == 0), stop=(ci == 15)),
                                sem="sT" if ci == 15 else None)
                    gst = 4 * (w % 2) + nt
                    gsl = gwin[:, TPC * gst + 512 * tb: TPC * gst + 512 * (tb + 1)]
                    if kind == "g":
                        kb.wait("scalar", "sT")
                        evac_tgt[gi] = kb.step(
                            "scalar", lambda s, gsl=gsl, j=j: s.activation(
                                out=gsl, in_=ps_gu[j][:],
                                func=AF.Gelu_apprx_tanh), sem="sS")
                    else:
                        kb.wait("vector", "sT")
                        evac_tgt[gi] = kb.step(
                            "vector", lambda v, gsl=gsl, j=j: v.tensor_tensor(
                                out=gsl, in0=gsl, in1=ps_gu[j][:], op=ALU.mult),
                            sem="sV")
                    gi += 1
                    pop_jobs(pi + 1, 3)
            prev_gu = (evac_tgt, evsem, 8)
        else:
            evac_tgt = {}
            gi = 0
            for hc in range(16):
                for tb in range(2):
                    j = gi % 4
                    if gi >= 4:
                        kb.wait("tensor", "sV", evac_tgt[gi - 4])
                    elif prev_d is not None:
                        pv_evac, pv_sem, pv_n = prev_d
                        kb.wait("tensor", pv_sem, pv_evac[pv_n - 4 + gi])
                    for sub in range(8):
                        w_ap = w4buf[slot][:, sub * E + 128 * hc:
                                           sub * E + 128 * (hc + 1)].bitcast(BF16)
                        h_ap = gwin[:, TPC * sub + 512 * tb:
                                    TPC * sub + 512 * (tb + 1)]
                        kb.step("tensor", lambda t, wv=w_ap, h=h_ap, sub=sub, j=j:
                                t.matmul(out=ps_dn[j][:], lhsT=wv, rhs=h,
                                         start=(sub == 0), stop=(sub == 7)),
                                sem="sT" if sub == 7 else None)
                    kb.wait("vector", "sT")
                    fpar = gi % 2
                    stsem = "qG" if fpar == 0 else "qH"
                    kb.wait("vector", stsem)
                    evac_tgt[gi] = kb.step(
                        "vector", lambda v, fpar=fpar, j=j: v.tensor_copy(
                            out=fstage[:, 512 * fpar:512 * (fpar + 1)],
                            in_=ps_dn[j][:]), sem="sV")
                    kb.wait("gpsimd", "sV", evac_tgt[gi])
                    dst = acc_dram[128 * hc:128 * (hc + 1),
                                   512 * tb:512 * (tb + 1)]
                    if w == 0:
                        kb.step("gpsimd", lambda g2, dst=dst, fpar=fpar:
                                g2.dma_start(out=dst,
                                             in_=fstage[:, 512 * fpar:512 * (fpar + 1)]),
                                sem=stsem, k=16)
                    else:
                        kb.step("gpsimd", lambda g2, dst=dst, fpar=fpar:
                                g2.dma_start(out=dst,
                                             in_=fstage[:, 512 * fpar:512 * (fpar + 1)],
                                             accum_op=ALU.add), sem=stsem, k=16)
                    gi += 1
                    pop_jobs(pi + 1, 1)
            prev_d = (evac_tgt, "sV", 32)
        mm_tgt[pi] = kb.cnt["sT"]

    sync_fence()
    es4a.close()

    if DEBUG:
        acc_dbg = nc.dram_tensor("acc_dbg", [E, TPC], F32,
                                 kind="ExternalOutput").ap()
        kb.wait("gpsimd", "qG")
        kb.wait("gpsimd", "qH")
        kb.dma("gpsimd", acc_dbg, acc_dram, "qG")
        kb.wait("gpsimd", "qG")
        kb.barrier()

    acc_sb = es4.enter_context(nc.sbuf_tensor("acc_sb", [128, 16 * 512], F32))
    x1r = es4.enter_context(nc.sbuf_tensor("x1r", [128, 16 * 512], BF16))
    aBs = es4.enter_context(nc.sbuf_tensor("aBs", [128, 16 * 512], BF16))
    sq5 = es4.enter_context(nc.sbuf_tensor("sq5", [128, 16 * 512], BF16))
    mB5 = es4.enter_context(nc.sbuf_tensor("mB5", [128, 512], F32))
    rB5 = es4.enter_context(nc.sbuf_tensor("rB5", [128, 512], F32))
    ot1 = es4.enter_context(nc.sbuf_tensor("ot1", [128, 2 * 512], F32))
    ps_s5 = es4.enter_context(nc.psum_tensor("ps_s5", [1, 512], F32))
    ps_s25 = es4.enter_context(nc.psum_tensor("ps_s25", [1, 512], F32))
    ps_bc5 = es4.enter_context(nc.psum_tensor("ps_bc5", [128, 512], F32))

    for tb in range(2):
        if tb == 0:
            kb.wait("sync", "qG")
            kb.wait("sync", "qH")
        else:
            kb.wait("sync", "sV")
        x1sem = "qD" if tb == 0 else "qE"
        for ec in range(16):
            ldsem5 = "qL" if (ec % 2) == 0 else "qM"
            kb.dma("sync", acc_sb[:, 512 * ec:512 * (ec + 1)],
                   acc_dram[128 * ec:128 * (ec + 1), 512 * tb:512 * (tb + 1)],
                   ldsem5)
        for ec in range(16):
            kb.dma("sync", x1r[:, 512 * ec:512 * (ec + 1)],
                   x1b_sp[128 * ec:128 * (ec + 1), 512 * tb:512 * (tb + 1)],
                   x1sem)
        kb.wait("vector", "qL")
        kb.wait("vector", "qM")
        kb.wait("vector", "sT")
        cp_tgt = {}
        for ec in range(16):
            kb.step("vector", lambda v, ec=ec: v.tensor_copy(
                out=aBs[:, 512 * ec:512 * (ec + 1)],
                in_=acc_sb[:, 512 * ec:512 * (ec + 1)]), sem="sV")
            cp_tgt[ec] = kb.cnt["sV"]
        kb.wait("scalar", "sT")
        sq_tgt = {}
        for ec in range(16):
            kb.wait("scalar", "sV", cp_tgt[ec])
            kb.step("scalar", lambda s, ec=ec: s.activation(
                out=sq5[:, 512 * ec:512 * (ec + 1)],
                in_=aBs[:, 512 * ec:512 * (ec + 1)],
                func=AF.Square), sem="sS")
            sq_tgt[ec] = kb.cnt["sS"]
        for ec in range(16):
            kb.wait("tensor", "sV", cp_tgt[ec])
            kb.step("tensor", lambda t, ec=ec: t.matmul(
                out=ps_s5[0:1, :], lhsT=ones128[:],
                rhs=aBs[:, 512 * ec:512 * (ec + 1)],
                start=(ec == 0), stop=(ec == 15)))
            kb.wait("tensor", "sS", sq_tgt[ec])
            kb.step("tensor", lambda t, ec=ec: t.matmul(
                out=ps_s25[0:1, :], lhsT=ones128[:],
                rhs=sq5[:, 512 * ec:512 * (ec + 1)],
                start=(ec == 0), stop=(ec == 15)), sem="sT")
        kb.wait("vector", "sT")
        kb.step("vector", lambda v: v.tensor_scalar_mul(
            rw_m[0:1, 0:512], ps_s5[0:1, :], R_E))
        kb.step("vector", lambda v: v.tensor_tensor(
            out=rw_t[0:1, 0:512], in0=rw_m[0:1, 0:512], in1=rw_m[0:1, 0:512],
            op=ALU.mult))
        kb.step("vector", lambda v: v.scalar_tensor_tensor(
            out=rw_v[0:1, 0:512], in0=ps_s25[0:1, :], scalar=R_E,
            in1=rw_t[0:1, 0:512], op0=ALU.mult, op1=ALU.subtract), sem="sV")
        kb.wait("scalar", "sV")
        kb.step("scalar", lambda s: s.activation(
            out=rw_v[0:1, 0:512], in_=rw_v[0:1, 0:512], func=AF.Sqrt, bias=epsb[0:1, 0:1]),
            sem="sS")
        kb.wait("vector", "sS")
        kb.step("vector", lambda v: v.reciprocal(
            out=rw_v[0:1, 0:512], in_=rw_v[0:1, 0:512]), sem="sV")
        kb.wait("tensor", "sV")
        kb.step("tensor", lambda t: t.matmul(
            out=ps_bc5[:], lhsT=ones1f[:], rhs=rw_m[0:1, 0:512],
            start=True, stop=True), sem="sT")
        kb.wait("vector", "sT")
        kb.step("vector", lambda v: v.tensor_copy(out=mB5[:], in_=ps_bc5[:]),
                sem="sV")
        kb.wait("tensor", "sV")
        kb.step("tensor", lambda t: t.matmul(
            out=ps_bc5[:], lhsT=ones1f[:], rhs=rw_v[0:1, 0:512],
            start=True, stop=True), sem="sT")
        kb.wait("vector", "sT")
        kb.step("vector", lambda v: v.tensor_copy(out=rB5[:], in_=ps_bc5[:]),
                sem="sV")
        kb.wait("vector", x1sem)
        kb.wait("vector", "qG")
        kb.wait("vector", "qH")
        for ec in range(16):
            slot = 512 * (ec % 2)
            asl = acc_sb[:, 512 * ec:512 * (ec + 1)]
            if ec >= 2:
                stsem_p = "qG" if (ec % 2) == 0 else "qH"
                kb.wait("vector", stsem_p)
            kb.step("vector", lambda v, asl=asl, slot=slot: v.tensor_tensor(
                out=ot1[:, slot:slot + 512], in0=asl, in1=mB5[:],
                op=ALU.subtract))
            kb.step("vector", lambda v, slot=slot: v.tensor_tensor(
                out=ot1[:, slot:slot + 512], in0=ot1[:, slot:slot + 512],
                in1=rB5[:], op=ALU.mult))
            kb.step("vector", lambda v, slot=slot, ec=ec: v.tensor_tensor(
                out=ot1[:, slot:slot + 512], in0=ot1[:, slot:slot + 512],
                in1=x1r[:, 512 * ec:512 * (ec + 1)], op=ALU.add), sem="sV")
            stsem = "qG" if (ec % 2) == 0 else "qH"
            kb.wait("gpsimd", "sV")
            kb.dma("gpsimd",
                   out_d[128 * ec:128 * (ec + 1), 512 * tb:512 * (tb + 1)],
                   ot1[:, slot:slot + 512], stsem)

    kb.wait("gpsimd", "qG")
    kb.wait("gpsimd", "qH")

    with nc.Block() as block:
        block.sync(lambda e: [f(e) for f in kb.prog["sync"]])
        block.vector(lambda e: [f(e) for f in kb.prog["vector"]])
        block.scalar(lambda e: [f(e) for f in kb.prog["scalar"]])
        block.tensor(lambda e: [f(e) for f in kb.prog["tensor"]])
        block.gpsimd(lambda e: [f(e) for f in kb.prog["gpsimd"]])
    es4.close()
    es34.close()
    es.close()
    return nc



def kernel(**inputs):
    d = _hostprep(inputs)
    nc = build_program()
    in_maps = []
    for r in range(NC):
        in_maps.append(dict(
            x0T_own=d["x0T_own"][r].astype(np.float32),
            wqkvT=d["wqkvT"][r].astype(np.float32),
            wpT_own=d["wpT_own"][r].astype(np.float32),
            pk_g=d["pk_g"], pk_u=d["pk_u"], pk_d=d["pk_d"],
            scT_g=d["scT_g"].astype(np.float32),
            scT_u=d["scT_u"].astype(np.float32),
            scT_d=d["scT_d"].astype(np.float32),
            CS1=d["CS1"], CS2=d["CS2"],
            maskM=d["maskM"], ident=d["ident"],
        ))
    res = run_bass_kernel_spmd(nc, in_maps, core_ids=list(range(NC)))
    big = np.empty((E, T), dtype=np.float32)
    for r in range(NC):
        big[:, d["own_cols"][r]] = res.results[r]["out"]
    out = np.empty((T, E), dtype=np.float32)
    out[:, PI_E] = big.T
    return out[None]
